# version 10
# baseline (speedup 1.0000x reference)
import numpy as np
import ml_dtypes

import concourse.bass as bass
import concourse.mybir as mybir
from concourse.bass_utils import run_bass_kernel_spmd
from concourse.bacc import Bacc

F32 = mybir.dt.float32
BF16 = mybir.dt.bfloat16
I32 = mybir.dt.int32
AF = mybir.ActivationFunctionType
ALU = mybir.AluOpType
AX = mybir.AxisListType

from concourse.dve_op import DveOp
from concourse.dve_ops import (
    OPS as _DVE_OPS,
    CUSTOM_DVE_SPECS as _DVE_SPECS,
    CUSTOM_DVE_REFS as _DVE_REFS,
    _SUB_OPCODE_FOR_NAME as _DVE_ROWS,
    _CUSTOM_DVE_ROW_BASE as _DVE_ROW_BASE,
)
from concourse.dve_spec import (
    Spec, Src0, C0, C1, C2, One,
    lower as _dve_lower, AluOp as _DAlu, Bin as _DBin,
)

_GRID = np.array([0.0, 0.5, 1.0, 1.5, 2.0, 3.0, 4.0, 6.0])
_MID = np.array([0.25, 0.75, 1.25, 1.75, 2.5, 3.5, 5.0])
HC = float(np.float32(0.25) - np.float32(2.0 ** -23))
DENORM1 = float(np.frombuffer(np.uint32(0x003FFFFF).tobytes(), dtype=np.float32)[0])
DENORM2 = float(np.frombuffer(np.uint32(0x007FFFFF).tobytes(), dtype=np.float32)[0])
SGN_I32 = -2147483648


def _fp4_round_ref(in0, in1, s0, s1, imm2):
    a = np.abs(in0)
    idx = np.searchsorted(_MID, a, side="left")
    g = _GRID[idx]
    return g.astype(np.float64 if in0.dtype == np.float64 else np.float32)


def _make_fp4_op():
    c = Src0 < One
    v = Src0 + c
    e2 = _DBin(_DAlu.BITWISE_AND, v, _DBin(_DAlu.BITWISE_NOT, C1, C1))
    h = e2 * C2
    w = v + h
    g = _DBin(_DAlu.BITWISE_AND, w, _DBin(_DAlu.BITWISE_NOT, C0, C0))
    q = g - c
    spec = Spec(body=q)
    shas = {v_: _dve_lower(spec, ver=v_, name="FP4_ROUND_ANT").sha(v_)
            for v_ in ("v3", "v4")}
    return DveOp("FP4_ROUND_ANT", spec, subdim=False, uops_sha=shas,
                 reference=_fp4_round_ref)


if "FP4_ROUND_ANT" not in _DVE_ROWS:
    FP4_ROUND = _make_fp4_op()
    _DVE_OPS.append(FP4_ROUND)
    _DVE_SPECS[FP4_ROUND.name] = FP4_ROUND.spec
    _DVE_REFS[FP4_ROUND.name] = FP4_ROUND.ref
    _DVE_ROWS[FP4_ROUND.name] = _DVE_ROW_BASE + len(_DVE_OPS) - 1
else:
    FP4_ROUND = next(o for o in _DVE_OPS if o.name == "FP4_ROUND_ANT")

T = 2048
H = 1024
II = 2816
NPAIR = 8
NTT = 16
KC = 16
CHUNKS = [512] * 5 + [256]
CHUNK_OFF = [0, 512, 1024, 1536, 2048, 2560]


class Prog:
    ENGINES = ("sync", "gpsimd", "vector", "scalar", "tensor")

    def __init__(self, nc):
        self.nc = nc
        self.entries = []
        self.sems = {}
        self.cnt = {}

    def sem(self, name):
        if name not in self.sems:
            self.sems[name] = self.nc.alloc_semaphore(name)
            self.cnt[name] = 0
        return self.sems[name]

    def add(self, eng, fn, waits=(), inc=None, inc_k=1):
        waits = [(s, v) for (s, v) in waits if v > 0]
        for (s, _v) in waits:
            self.sem(s)
        self.entries.append((eng, fn, waits, inc, inc_k))
        if inc is not None:
            self.sem(inc)
            self.cnt[inc] += inc_k
            return self.cnt[inc]
        return None

    def now(self, name):
        self.sem(name)
        return self.cnt[name]

    def emit(self):
        with self.nc.Block() as block:
            for engname in self.ENGINES:
                entries = [e for e in self.entries if e[0] == engname]
                if not entries:
                    continue

                def section(eng, entries=entries):
                    last = {}
                    for (_, fn, waits, inc, inc_k) in entries:
                        for (semname, val) in waits:
                            if last.get(semname, 0) >= val:
                                continue
                            eng.wait_ge(self.sems[semname], val)
                            last[semname] = val
                        ins = fn(eng)
                        if inc is not None:
                            ins.then_inc(self.sems[inc], inc_k)
                getattr(block, engname)(section)


def build(debug=False, upto="all", nheads=16, nchunks=None):
    nc = Bacc("TRN2", target_bir_lowering=False, debug=False,
              detect_race_conditions=False)

    x_in = nc.dram_tensor("x_in", [T, H], F32, kind="ExternalInput")
    wqT_h = nc.dram_tensor("wqT", [H, H], F32, kind="ExternalInput")
    wkT_h = nc.dram_tensor("wkT", [H, H], F32, kind="ExternalInput")
    wvT_h = nc.dram_tensor("wvT", [H, H], F32, kind="ExternalInput")
    woT_h = nc.dram_tensor("woT", [H, H], F32, kind="ExternalInput")
    wg_h = nc.dram_tensor("wg", [II, H], F32, kind="ExternalInput")
    wu_h = nc.dram_tensor("wu", [II, H], F32, kind="ExternalInput")
    wd_h = nc.dram_tensor("wd", [H, II], F32, kind="ExternalInput")
    ident_h = nc.dram_tensor("ident", [128, 128], BF16, kind="ExternalInput")
    y_out = nc.dram_tensor("y", [T, H], F32, kind="ExternalOutput")
    wgq_hbm = nc.dram_tensor("wgq_s", [128, 8, II], BF16)
    wuq_hbm = nc.dram_tensor("wuq_s", [128, 8, II], BF16)
    wdq_hbm = nc.dram_tensor("wdq_s", [22, 128, H], BF16)
    xqT_hbm = nc.dram_tensor("xqT_s", [128, 8, T], BF16)
    xql_hbm = nc.dram_tensor("xql_s", [128, 8, T], BF16)
    wgql_hbm = nc.dram_tensor("wgql_s", [128, 8, II], BF16)
    wuql_hbm = nc.dram_tensor("wuql_s", [128, 8, II], BF16)
    ao_hbm = nc.dram_tensor("ao_s", [T, H], BF16)

    def sb(name, shape, dtype, off):
        return nc.alloc_sbuf_tensor_at(name, shape, dtype, offset=off)

    F2O = 18432
    AO = 34816
    BO = 67584
    CO = 100352
    DO = 133120
    EO = 198656
    HO = 210944

    xnT = sb("xnT", [128, 8, T], BF16, AO)
    xqlT = sb("xqlT", [128, 8, T], BF16, AO)
    xqT = sb("xqT", [128, 8, T], BF16, BO)
    QT_sb = sb("QT_sb", [128, 8, T], BF16, BO)
    KT_sb = sb("KT_sb", [128, 8, T], BF16, CO)
    wgq_sb = sb("wgq", [128, 8, 512], BF16, CO)
    wgql_sb = sb("wgql", [128, 8, 512], BF16, CO + 8192)
    wuq_sb = sb("wuq", [128, 8, 512], BF16, CO + 16384)
    wuql_sb = sb("wuql", [128, 8, 512], BF16, CO + 24576)
    ob_t = [sb(f"ob_t{b}", [128, H], F32, CO + b * 4096) for b in range(2)]
    ao_t = [sb(f"ao_t{b}", [128, H], BF16, EO + 8192 + b * 2048) for b in range(2)]
    Vtm = sb("Vtm", [128, 16, 1040], BF16, DO)
    wb2 = sb("wb2", [128, 8, H], BF16, DO + 33280)
    woT_sb = sb("woT_sb", [128, 8, H], BF16, DO + 33280)
    ffn_acc = sb("ffn_acc", [128, NTT * H], F32, DO)
    mirQ = [sb(f"mirQ{b}", [128, 8, 128], BF16, EO + b * 2048) for b in range(2)]
    aT_h = [sb(f"aTh{b}", [128, 8, 128], BF16, EO + 4096 + b * 2048)
            for b in range(2)]
    araw = sb("araw", [64, 1024], BF16, EO + 8192)
    Vf = sb("Vf", [128, H], BF16, EO + 8192)
    tnorm = sb("tnorm", [128, 512], BF16, EO + 10240)
    P_A = [sb(f"P_A{b}", [128, 1024], BF16, F2O + b * 2048) for b in range(2)]
    P_B = [sb(f"P_B{b}", [128, 1024], BF16, F2O + 4096 + b * 2048) for b in range(2)]
    Qsqbuf = sb("Qsqbuf", [128, T], BF16, F2O)
    xsq_f = sb("xsq_f", [128, H], F32, DO)
    wA_f = sb("wA_f", [128, 1408], F32, DO)
    x_t = [sb(f"x_t{b}", [128, H], F32, EO + b * 4096) for b in range(2)]
    xn_f = sb("xn_f", [128, H], F32, EO + 8192)
    wM = sb("wM", [128, 96], F32, EO)
    wS = sb("wS", [128, 96], F32, EO + 512)
    wR = sb("wR", [128, 96], F32, EO + 1024)
    wq_b16 = sb("wq_b16", [128, 1408], BF16, EO + 1536)
    wstage = sb("wstage", [128, 11, 128], BF16, EO + 4352)
    sg_sb = sb("sg_sb", [128, 512], F32, EO)
    h_sb = sb("h_sb", [128, 512], F32, EO + 2048)
    hq_sb = sb("hq_sb", [128, 512], BF16, EO + 4096)
    hqT_sb = sb("hqT_sb", [128, 512], BF16, EO + 5120)
    hT_sb = sb("hT_sb", [128, 512], F32, EO + 6144)
    hA_sb = sb("hA_sb", [128, 512], F32, EO + 8192)
    hQ_sb = sb("hQ_sb", [128, 512], BF16, EO + 10240)
    hM = sb("hM", [128, 32], F32, EO + 11264)
    hS = sb("hS", [128, 32], F32, EO + 11392)
    hR = sb("hR", [128, 32], F32, EO + 11520)
    xr_t = [sb(f"xr_t{b}", [128, H], F32, EO + b * 4096) for b in range(2)]
    b0s1 = sb("b0s1", [128, 1], F32, F2O)
    b0s2 = sb("b0s2", [128, 1], F32, F2O + 32)
    b0m = sb("b0m", [128, 1], F32, F2O + 64)
    b0msq = sb("b0msq", [128, 1], F32, F2O + 96)
    b0var = sb("b0var", [128, 1], F32, F2O + 128)
    b0ln = sb("b0ln", [128, 1], F32, F2O + 160)
    b0rstd = sb("b0rstd", [128, 1], F32, F2O + 192)
    b0nmr = sb("b0nmr", [128, 1], F32, F2O + 224)
    b0M = sb("b0M", [128, 64], F32, F2O + 256)
    b0S = sb("b0S", [128, 64], F32, F2O + 512)
    b0R = sb("b0R", [128, 64], F32, F2O + 768)
    b0T = sb("b0T", [128, H], F32, F2O + 1024)
    b0A = sb("b0A", [128, H], F32, F2O + 5120)
    b0Q = sb("b0Q", [128, H], BF16, F2O + 9216)
    xn_b = sb("xn_b", [128, H], BF16, F2O + 11264)
    xq_b = sb("xq_b", [128, H], BF16, F2O + 13312)
    lABb = sb("lABb", [33, 1024], BF16, F2O + 8192)
    lrecf = sb("lrecf", [65, 1024], F32, F2O + 10240)
    mqs = sb("mqs", [34, 16, 16], F32, F2O + 14336)
    mqMX = sb("mqMX", [34, 16], F32, F2O + 15360)
    mqMXb = sb("mqMXb", [34, 16], BF16, F2O + 15424)
    cpr = sb("cpr", [1, 16], F32, F2O + 15456)
    cpos = sb("cpos", [1, 16], BF16, F2O + 15520)
    biasrow = sb("biasrow", [128, 16], F32, F2O + 15552)
    rT_f = sb("rT_f", [128, 8], F32, F2O + 15616)
    mqMX2 = sb("mqMX2", [34, 16], F32, F2O + 15648)
    wbuf = [sb(f"wbuf{b}", [128, 1408], F32, CO + b * 5632) for b in range(2)]
    wT_f = sb("wT_f", [128, 1408], F32, CO + 11264)
    wQ_b = sb("wQ_b", [128, 1408], BF16, CO + 16896)
    wdq_sb = [sb(f"wdq{b}", [128, 4, H], BF16, F2O + b * 8192) for b in range(2)]
    ident_s = sb("ident_s", [128, 128], BF16, HO)
    onesneg = sb("onesneg", [33, 128], BF16, HO + 256)
    ones1 = sb("ones1", [128, 1], BF16, HO + 512)
    bo = sb("bo", [128, 33], BF16, HO + 544)
    sgnm = sb("sgnm", [128, 1], I32, HO + 640)
    bo2 = sb("bo2", [128, 34], BF16, HO + 672)

    PS = [nc.alloc_psum_tensor(f"PS{i}", [128, 1024], F32) for i in range(4)]
    PSb = [p[:, :].bitcast(BF16) for p in PS]

    P = Prog(nc)

    _taps = []
    def tapnow(name, ap, waits):
        if not debug:
            return
        sh = list(ap.shape)
        t = nc.dram_tensor(f"dbg_{name}", sh, ap.dtype, kind="ExternalOutput")
        P.add("gpsimd", lambda e: e.dma_start(t[tuple(slice(0, s) for s in sh)], ap),
              waits=list(waits), inc="gdbg", inc_k=16)
        _taps.append(name)

    P.add("sync", lambda e: e.dma_start(ident_s[:, :], ident_h[:, :]),
          inc="q_c", inc_k=16)
    for fn in (lambda e: e.memset(onesneg[:, :], -1.0),
               lambda e: e.memset(ones1[:, :], 1.0),
               lambda e: e.memset(bo[:, :], 0.0),
               lambda e: e.memset(bo[0:64, 0:1], 1.0),
               lambda e: e.memset(bo[64:128, 32:33], 1.0),
               lambda e: e.memset(bo2[:, :], 0.0),
               lambda e: e.memset(bo2[0:64, 1:2], 1.0),
               lambda e: e.memset(bo2[64:128, 33:34], 1.0),
               lambda e: e.memset(sgnm[:, :], SGN_I32)):
        P.add("gpsimd", fn, inc="g_i")
    GI = P.now("g_i")
    QC = P.now("q_c")

    def qd(src, dst, Mb, Sb, Rb, Tb, Ab, Qb, F, waits):
        nb = F // 16
        VM_ = P.add("vector", lambda e: e.tensor_reduce(
            Mb[:, 0:nb].unsqueeze(2),
            src[:, 0:F].rearrange("p (s b) -> p s b", b=16),
            AX.X, ALU.max, apply_absolute_value=True), waits=waits, inc="vp")
        SS_ = P.add("scalar", lambda e: e.activation(
            Sb[:, 0:nb], Mb[:, 0:nb], AF.Copy, scale=1.0 / 6.0, bias=1e-12),
            waits=[("vp", VM_)], inc="sp")
        P.add("vector", lambda e: e.reciprocal(Rb[:, 0:nb], Sb[:, 0:nb]),
              waits=[("sp", SS_)])
        P.add("vector", lambda e: e.tensor_tensor(
            Tb[:, 0:F].rearrange("p (s b) -> p s b", b=16),
            src[:, 0:F].rearrange("p (s b) -> p s b", b=16),
            Rb[:, 0:nb].unsqueeze(2).broadcast_to([128, nb, 16]), ALU.mult))
        P.add("vector", lambda e: e.tensor_single_scalar(
            Ab[:, 0:F].bitcast(I32), Tb[:, 0:F].bitcast(I32), 2147483647,
            ALU.bitwise_and))
        P.add("vector", lambda e: e._custom_dve(
            FP4_ROUND, out=Qb[:, 0:F], in0=Ab[:, 0:F],
            s0=DENORM1, s1=DENORM2, imm2=HC))
        P.add("vector", lambda e: e.tensor_tensor(
            Ab[:, 0:F].rearrange("p (s b) -> p s b", b=16),
            Qb[:, 0:F].rearrange("p (s b) -> p s b", b=16),
            Sb[:, 0:nb].unsqueeze(2).broadcast_to([128, nb, 16]), ALU.mult))
        P.add("vector", lambda e: e.scalar_tensor_tensor(
            src[:, 0:F].bitcast(I32), Tb[:, 0:F].bitcast(I32), sgnm[:, :],
            Ab[:, 0:F].bitcast(I32), ALU.bitwise_and, ALU.bitwise_or))
        return P.add("vector", lambda e: e.tensor_copy(dst, src[:, 0:F]),
                     inc="vp")


    def _early_out(tag):
        P.add("gpsimd", lambda e: e.memset(ob_t[0][:, :], 0.0),
              waits=[("sp", P.now("sp")), ("vp", P.now("vp")),
                     ("tp", P.now("tp")), ("gw", P.now("gw"))], inc="g_i")
        GE = P.now("g_i")
        for tt in range(NTT):
            P.add("gpsimd", lambda e, tt=tt: e.dma_start(
                y_out[tt * 128:(tt + 1) * 128, :], ob_t[0][:, :]),
                waits=[("g_i", GE)], inc="g_st", inc_k=16)
        P.add("gpsimd", lambda e: e.wait_ge(P.sems["g_st"], NTT * 16))
        P.emit()
        nc.compile()
        return nc
    b0_evac = [0] * NTT
    b0_trans = [0] * NTT
    b0_xnact = [0] * NTT
    b0_srstd = [0] * NTT
    b0_loev = [0] * NTT
    b0_lodma = [0] * NTT
    for tt in range(NTT):
        b = tt % 2
        w = [("q_c", QC)]
        if tt >= 2:
            w.append(("vp", b0_xnact[tt - 2]))
        P.add("sync", lambda e, tt=tt, b=b: e.dma_start(
            x_t[b][:, :], x_in[tt * 128:(tt + 1) * 128, :]),
            waits=w, inc="q_x", inc_k=16)
        QX = P.now("q_x")
        wv0 = [("q_x", QX), ("g_i", GI)]
        if tt >= 1:
            wv0.append(("sp", b0_srstd[tt - 1]))
        VS1 = P.add("vector", lambda e, b=b: e.reduce_sum(
            b0s1[:, :], x_t[b][:, :], AX.X), waits=wv0, inc="vp")
        SM = P.add("scalar", lambda e: e.activation(
            b0m[:, :], b0s1[:, :], AF.Copy, scale=1.0 / H),
            waits=[("vp", VS1)], inc="sp")
        P.add("vector", lambda e, b=b: e.tensor_scalar(
            xsq_f[:, :], x_t[b][:, :], b0m[:, :], None, ALU.subtract),
            waits=[("sp", SM)])
        P.add("vector", lambda e: e.tensor_tensor(
            b0T[:, :], xsq_f[:, :], xsq_f[:, :], ALU.mult))
        VV = P.add("vector", lambda e: e.reduce_sum(
            b0s2[:, :], b0T[:, :], AX.X), inc="vp")
        P.add("scalar", lambda e: e.activation(
            b0var[:, :], b0s2[:, :], AF.Copy, bias=1e-5, scale=1.0 / H),
            waits=[("vp", VV)])
        P.add("scalar", lambda e: e.activation(b0ln[:, :], b0var[:, :], AF.Ln))
        SR = P.add("scalar", lambda e: e.activation(
            b0rstd[:, :], b0ln[:, :], AF.Exp, scale=-0.5), inc="sp")
        b0_srstd[tt] = SR
        SX = P.add("vector", lambda e: e.tensor_scalar(
            xn_f[:, :], xsq_f[:, :], b0rstd[:, :], None, ALU.mult),
            waits=[("sp", SR)], inc="vp")
        b0_xnact[tt] = SX
        wtr = []
        if tt >= 1:
            wtr.append(("tp", b0_trans[tt - 1]))
            wtr.append(("sp", b0_loev[tt - 1]))
            wtr.append(("gw", b0_lodma[tt - 1]))
        VC = P.add("vector", lambda e: e.tensor_copy(xn_b[:, :], xn_f[:, :]),
                   waits=wtr, inc="vp")
        VQ = qd(xn_f, xq_b[:, :], b0M, b0S, b0R, b0T, b0A, b0Q, H, waits=[])
        for hc in range(8):
            w_t = []
            if hc == 0:
                w_t = [("vp", VC), ("q_c", QC)]
                if tt >= 1:
                    w_t.append(("sp", b0_evac[tt - 1]))
            P.add("tensor", lambda e, hc=hc: e.transpose(
                PSb[0][:, hc * 128:(hc + 1) * 128],
                xn_b[:, hc * 128:(hc + 1) * 128], ident_s[:, :]),
                waits=w_t, inc="tp" if hc == 7 else None)
        for hc in range(8):
            w_t = [("vp", VQ)] if hc == 0 else []
            if hc == 0 and tt >= 1:
                w_t.append(("sp", b0_loev[tt - 1]))
            P.add("tensor", lambda e, hc=hc: e.transpose(
                PSb[1][:, hc * 128:(hc + 1) * 128],
                xq_b[:, hc * 128:(hc + 1) * 128], ident_s[:, :]),
                waits=w_t, inc="tp" if hc == 7 else None)
        b0_trans[tt] = P.now("tp")
        TP = P.now("tp")
        P.add("scalar", lambda e, tt=tt: e.copy(
            xnT[:, 0:8, tt * 128:(tt + 1) * 128],
            PSb[0][:, 0:1024].rearrange("p (h q) -> p h q", q=128)),
            waits=[("tp", TP)])
        b0_evac[tt] = P.add("scalar", lambda e, tt=tt: e.copy(
            xqT[:, 0:8, tt * 128:(tt + 1) * 128],
            PSb[1][:, 0:1024].rearrange("p (h q) -> p h q", q=128)), inc="sp")
        P.add("vector", lambda e: e.tensor_copy(b0T[:, :], xq_b[:, :]))
        P.add("vector", lambda e: e.tensor_tensor(
            b0A[:, :], xn_f[:, :], b0T[:, :], ALU.subtract))
        VLB = P.add("vector", lambda e: e.tensor_copy(xq_b[:, :], b0A[:, :]),
                    waits=[("tp", b0_trans[tt])], inc="vp")
        for hc in range(8):
            w_t = [("vp", VLB), ("sp", b0_evac[tt])] if hc == 0 else []
            P.add("tensor", lambda e, hc=hc: e.transpose(
                PSb[1][:, hc * 128:(hc + 1) * 128],
                xq_b[:, hc * 128:(hc + 1) * 128], ident_s[:, :]),
                waits=w_t, inc="tp" if hc == 7 else None)
        TLO = P.now("tp")
        b0_loev[tt] = P.add("scalar", lambda e: e.copy(
            xn_b[:, 0:1024], PSb[1][:, 0:1024]),
            waits=[("tp", TLO)], inc="sp")
        b0_lodma[tt] = P.add("gpsimd", lambda e, tt=tt: e.dma_start(
            xql_hbm[:, 0:8, tt * 128:(tt + 1) * 128],
            xn_b[:, 0:1024].rearrange("p (h q) -> p h q", q=128)),
            waits=[("sp", b0_loev[tt])], inc="gw", inc_k=16)
        if tt == 0:
            tapnow("x_t0", x_t[0][:, :], [("vp", VV)])
            tapnow("b0s1", b0s1[:, :], [("vp", VV)])
            tapnow("b0s2", b0s2[:, :], [("vp", VV)])
            tapnow("b0var", b0var[:, :], [("vp", VV)])
            tapnow("b0rstd", b0rstd[:, :], [("sp", SR)])
            tapnow("xn_f0", xn_f[:, :], [("vp", SX)])
            tapnow("xn_b0", xn_b[:, :], [("vp", VC)])
            tapnow("xq_b0", xq_b[:, :], [("vp", VQ)])
            tapnow("xnT00", xnT[:, 0:8, 0:128], [("sp", b0_evac[0])])
    B0_S = P.now("sp")
    tapnow("xnT", xnT[:, 0:8, 0:256], [("sp", B0_S)])
    tapnow("xqT", xqT[:, 0:8, 0:256], [("sp", B0_S)])
    if upto == "b0":
        return _early_out("b0")

    items = ([("g", it) for it in range(22)] + [("u", it) for it in range(22)]
             + [("d", hc, hf) for hc in range(8) for hf in range(2)])
    NIT = len(items)
    b1_vq = [0] * NIT
    b1_lo = [0] * NIT
    b1_tp = [0] * NIT
    b1_se = [0] * NIT
    b1_gw = [0] * NIT
    for ii, item in enumerate(items):
        b = ii % 2
        F = 1024 if item[0] in ("g", "u") else 1408
        nt = F // 128
        wdma = [("sp", B0_S)]
        if ii >= 2:
            wdma.append(("vp", b1_lo[ii - 2]))
        if item[0] in ("g", "u"):
            src_h = wg_h if item[0] == "g" else wu_h
            it = item[1]
            P.add("sync", lambda e, b=b, src_h=src_h, it=it: e.dma_start(
                wbuf[b][:, 0:1024], src_h[it * 128:(it + 1) * 128, :]),
                waits=wdma, inc="q_w", inc_k=16)
        else:
            hc, hf = item[1], item[2]
            P.add("sync", lambda e, b=b, hc=hc, hf=hf: e.dma_start(
                wbuf[b][:, 0:1408],
                wd_h[hc * 128:(hc + 1) * 128, hf * 1408:(hf + 1) * 1408]),
                waits=wdma, inc="q_w", inc_k=16)
        QW = P.now("q_w")
        wq_waits = [("q_w", QW)]
        if ii >= 1:
            wq_waits.append(("tp", b1_tp[ii - 1]))
        b1_vq[ii] = qd(wbuf[b], wq_b16[:, 0:F], wM, wS, wR, wT_f, wA_f, wQ_b,
                       F, waits=wq_waits)
        for j in range(min(nt, 8)):
            w_t = []
            if j == 0:
                w_t = [("vp", b1_vq[ii])]
                if ii >= 1:
                    w_t.append(("sp", b1_se[ii - 1]))
            P.add("tensor", lambda e, j=j: e.transpose(
                PSb[2][:, j * 128:(j + 1) * 128],
                wq_b16[:, j * 128:(j + 1) * 128], ident_s[:, :]),
                waits=w_t, inc="tp" if j == min(nt, 8) - 1 else None)
        TP1 = P.now("tp")
        wse1 = [("tp", TP1)]
        if ii >= 1:
            wse1.append(("gw", b1_gw[ii - 1]))
        SE1 = P.add("scalar", lambda e, nt=nt: e.copy(
            wstage[:, 0:min(nt, 8), :],
            PSb[2][:, 0:min(nt, 8) * 128].rearrange("p (j q) -> p j q", q=128)),
            waits=wse1, inc="sp")
        if nt > 8:
            for j in range(8, nt):
                w_t = [("sp", SE1)] if j == 8 else []
                P.add("tensor", lambda e, j=j: e.transpose(
                    PSb[2][:, (j - 8) * 128:(j - 7) * 128],
                    wq_b16[:, j * 128:(j + 1) * 128], ident_s[:, :]),
                    waits=w_t, inc="tp" if j == nt - 1 else None)
            TP2 = P.now("tp")
            SE2 = P.add("scalar", lambda e, nt=nt: e.copy(
                wstage[:, 8:nt, :],
                PSb[2][:, 0:(nt - 8) * 128].rearrange("p (j q) -> p j q", q=128)),
                waits=[("tp", TP2)], inc="sp")
        else:
            SE2 = SE1
        b1_tp[ii] = P.now("tp")
        b1_se[ii] = SE2
        if item[0] in ("g", "u"):
            dsth = wgq_hbm if item[0] == "g" else wuq_hbm
            it = item[1]
            GWH = P.add("gpsimd", lambda e, dsth=dsth, it=it: e.dma_start(
                dsth[:, 0:8, it * 128:(it + 1) * 128], wstage[:, 0:8, :]),
                waits=[("sp", SE2)], inc="gw", inc_k=16)
            P.add("vector", lambda e: e.tensor_copy(wT_f[:, 0:1024], wq_b16[:, 0:1024]))
            b1_lo[ii] = P.add("vector", lambda e, b=b: e.tensor_tensor(
                wA_f[:, 0:1024], wbuf[b][:, 0:1024], wT_f[:, 0:1024],
                ALU.subtract), inc="vp")
            VW3 = P.add("vector", lambda e: e.tensor_copy(
                wq_b16[:, 0:1024], wA_f[:, 0:1024]),
                waits=[("tp", b1_tp[ii])], inc="vp")
            for j in range(8):
                w_t = [("vp", VW3), ("sp", SE2)] if j == 0 else []
                P.add("tensor", lambda e, j=j: e.transpose(
                    PSb[2][:, j * 128:(j + 1) * 128],
                    wq_b16[:, j * 128:(j + 1) * 128], ident_s[:, :]),
                    waits=w_t, inc="tp" if j == 7 else None)
            b1_tp[ii] = P.now("tp")
            SE3 = P.add("scalar", lambda e: e.copy(
                wstage[:, 0:8, :],
                PSb[2][:, 0:1024].rearrange("p (j q) -> p j q", q=128)),
                waits=[("tp", b1_tp[ii]), ("gw", GWH)], inc="sp")
            b1_se[ii] = SE3
            dstl = wgql_hbm if item[0] == "g" else wuql_hbm
            b1_gw[ii] = P.add("gpsimd", lambda e, dstl=dstl, it=it: e.dma_start(
                dstl[:, 0:8, it * 128:(it + 1) * 128], wstage[:, 0:8, :]),
                waits=[("sp", SE3)], inc="gw", inc_k=16)
        else:
            b1_lo[ii] = b1_vq[ii]
            hc, hf = item[1], item[2]
            b1_gw[ii] = P.add("gpsimd", lambda e, hc=hc, hf=hf: e.dma_start(
                wdq_hbm[hf * 11:(hf + 1) * 11, :, hc * 128:(hc + 1) * 128].transpose([1, 0, 2]),
                wstage[:, 0:11, :]),
                waits=[("sp", SE2)], inc="gw", inc_k=16)
    B1_G = P.now("gw")
    B1_S = P.now("sp")
    B1_V = P.now("vp")
    B1_T = P.now("tp")
    if upto == "b1":
        return _early_out("b1")

    QSC = float(8.0 ** -0.5)

    P.add("gpsimd", lambda e: e.dma_start(xqT_hbm[:, 0:8, :], xqT[:, 0:8, :]),
          waits=[("sp", B0_S)], inc="gw", inc_k=16)
    GW_XQ = P.now("gw")
    P.add("gpsimd", lambda e: e.memset(lABb[:, :], 0.0),
          waits=[("sp", B0_S), ("vp", B1_V)], inc="g_i")
    GI_L = P.now("g_i")
    P.add("gpsimd", lambda e: e.memset(
        Vtm[:, :, :].rearrange("p s (c d) -> p s c d", d=65)[:, :, :, 64:65], 1.0),
        inc="g_i")
    GI_V = P.now("g_i")

    proj_tp = [0, 0, 0]
    QT_DONE = 0
    for m_i, wh in enumerate((wqT_h, wkT_h, wvT_h)):
        w_l = [("gw", B1_G)]
        if m_i >= 1:
            w_l.append(("tp", proj_tp[m_i - 1]))
        P.add("gpsimd", lambda e, wh=wh: e.dma_start(
            wb2[:, 0:8, :], wh[:, :].rearrange("(c p) o -> p c o", p=128)),
            waits=w_l, inc="gw", inc_k=16)
        GW_M = P.now("gw")
        first = True
        for s8 in range(8):
            for half in range(2):
                for sl in range(2):
                    for hcc in range(8):
                        w_s = []
                        if sl == 0 and hcc == 0:
                            w_s = [("sp", P.now("sp"))]
                            if first:
                                w_s += [("gw", GW_M), ("sp", B0_S), ("sp", B1_S)]
                                first = False
                        P.add("tensor", lambda e, hcc=hcc, s8=s8, half=half, sl=sl: e.matmul(
                            PS[0][:, sl * 512:(sl + 1) * 512],
                            wb2[:, hcc, s8 * 128:(s8 + 1) * 128],
                            xnT[:, hcc, (half * 2 + sl) * 512:(half * 2 + sl + 1) * 512],
                            start=(hcc == 0), stop=(hcc == 7),
                            skip_group_check=True),
                            waits=w_s, inc="tp" if (sl == 1 and hcc == 7) else None)
                TPH = P.now("tp")
                if m_i < 2:
                    dst = QT_sb if m_i == 0 else KT_sb
                    w_e = [("tp", TPH)]
                    if m_i == 0 and s8 == 0 and half == 0:
                        w_e.append(("gw", GW_XQ))
                    if m_i == 1 and s8 == 0 and half == 0:
                        w_e += [("vp", B1_V), ("tp", B1_T)]
                    P.add("scalar", lambda e, dst=dst, s8=s8, half=half: e.activation(
                        dst[:, s8, half * 1024:(half + 1) * 1024],
                        PS[0][:, 0:1024], AF.Copy, scale=QSC),
                        waits=w_e, inc="sp")
                else:
                    w_e = [("tp", TPH)]
                    if s8 == 0 and half == 0:
                        w_e.append(("vp", B1_V))
                    SVF = P.add("scalar", lambda e: e.copy(
                        Vf[:, 0:1024], PS[0][:, 0:1024]), waits=w_e, inc="sp")
                    for st in range(8):
                        w_t = [("sp", SVF)] if st == 0 else []
                        P.add("tensor", lambda e, st=st: e.transpose(
                            PSb[1][:, st * 128:(st + 1) * 128],
                            Vf[:, st * 128:(st + 1) * 128], ident_s[:, :]),
                            waits=w_t, inc="tp" if st == 7 else None)
                    TVT = P.now("tp")
                    P.add("scalar", lambda e, s8=s8, half=half: e.copy(
                        Vtm[:, half * 8:(half + 1) * 8,
                            2 * s8 * 65:2 * s8 * 65 + 64],
                        PSb[1][:, 0:1024].rearrange("p (j q) -> p j q", q=128)[:, :, 0:64]),
                        waits=[("tp", TVT)])
                    P.add("scalar", lambda e, s8=s8, half=half: e.copy(
                        Vtm[:, half * 8:(half + 1) * 8,
                            (2 * s8 + 1) * 65:(2 * s8 + 1) * 65 + 64],
                        PSb[1][:, 0:1024].rearrange("p (j q) -> p j q", q=128)[:, :, 64:128]),
                        inc="sp")
        proj_tp[m_i] = P.now("tp")
        if m_i == 0:
            QT_DONE = P.now("sp")
    PROJ_S = P.now("sp")
    if debug:
        tapnow("QT", QT_sb[:, 0:8, 0:256], [("sp", QT_DONE)])
        tapnow("KT", KT_sb[:, 0:8, 0:256], [("sp", PROJ_S)])
        tapnow("Vtm", Vtm[:, 0:2, 0:1024], [("sp", PROJ_S)])
    if upto == "proj":
        return _early_out("proj")

    P.add("gpsimd", lambda e: e.dma_start(
        woT_sb[:, 0:8, :], woT_h[:, :].rearrange("(c p) o -> p c o", p=128)),
        waits=[("tp", proj_tp[2])], inc="gw", inc_k=16)
    GW_WO = P.now("gw")

    P.add("gpsimd", lambda e: e.memset(mqs[:, :, :], 0.0),
          waits=[("sp", B0_S)], inc="g_i")
    GI_M = P.now("g_i")
    for qk, SRC, bom in ((0, QT_sb, bo), (1, KT_sb, bo2)):
        nr = 33 + qk
        for t in range(8):
            w_q = [("sp", QT_DONE if qk == 0 else PROJ_S)]
            if qk > 0 or t > 0:
                w_q.append(("tp", P.now("tp")))
            VSQ = P.add("vector", lambda e, SRC=SRC, t=t: e.tensor_tensor(
                Qsqbuf[:, :], SRC[:, t, :], SRC[:, t, :], ALU.mult),
                waits=w_q, inc="vp")
            for r in range(2):
                for sl in range(2):
                    w_cm = []
                    if sl == 0:
                        w_cm = [("vp", VSQ if r == 0 else P.now("vp"))]
                    P.add("tensor", lambda e, r=r, sl=sl, bom=bom, nr=nr: e.matmul(
                        PS[3][0:nr, sl * 512:(sl + 1) * 512], bom[:, 0:nr],
                        Qsqbuf[:, r * 1024 + sl * 512:r * 1024 + (sl + 1) * 512],
                        start=True, stop=True, skip_group_check=True),
                        waits=w_cm, inc="tp" if sl == 1 else None)
                TCM = P.now("tp")
                w_r = [("tp", TCM)]
                if qk == 0 and t == 0 and r == 0:
                    w_r.append(("g_i", GI_M))
                P.add("vector", lambda e, r=r, t=t, qk=qk, nr=nr: e.tensor_reduce(
                    mqs[0:nr, r * 8:(r + 1) * 8, qk * 8 + t:qk * 8 + t + 1],
                    PS[3][0:nr, 0:1024].rearrange("p (b q) -> p b q", q=128),
                    AX.X, ALU.max, apply_absolute_value=True),
                    waits=w_r, inc="vp")
    VMF = P.add("vector", lambda e: e.tensor_reduce(
        mqMX[0:34, 0:16].unsqueeze(2), mqs[0:34, 0:16, 0:16],
        AX.X, ALU.max, apply_absolute_value=True),
        waits=[("vp", P.now("vp"))], inc="vp")
    P.add("scalar", lambda e: e.activation(
        mqMX2[0:34, :], mqMX[0:34, :], AF.Copy, bias=1.0),
        waits=[("vp", VMF)])
    SMF = P.add("scalar", lambda e: e.activation(
        mqMXb[0:34, :], mqMX2[0:34, :], AF.Ln), inc="sp")
    P.add("tensor", lambda e: e.matmul(
        PS[3][0:1, 0:16], ones1[0:34, 0:1], mqMXb[0:34, 0:16],
        start=True, stop=True, skip_group_check=True),
        waits=[("sp", SMF)], inc="tp")
    TSM = P.now("tp")
    SCP = P.add("scalar", lambda e: e.copy(cpr[0:1, :], PS[3][0:1, 0:16]),
                waits=[("tp", TSM)], inc="sp")
    SCS = P.add("scalar", lambda e: e.activation(
        cpos[0:1, :], cpr[0:1, :], AF.Exp, scale=0.25), inc="sp")
    P.add("tensor", lambda e: e.matmul(
        PS[3][:, 0:16], onesneg[0:1, :], cpos[0:1, 0:16],
        start=True, stop=True, skip_group_check=True),
        waits=[("sp", SCS)], inc="tp")
    TBR = P.now("tp")
    CB_S = P.add("scalar", lambda e: e.copy(biasrow[:, :], PS[3][:, 0:16]),
                 waits=[("tp", TBR)], inc="sp")
    if debug:
        tapnow("mqs", mqs[0:34, :, :], [("vp", VMF)])
        tapnow("mqMX", mqMX[0:34, :], [("sp", SMF)])
        tapnow("biasrow", biasrow[:, :], [("sp", CB_S)])
    if upto == "cbound":
        return _early_out("cbound")

    mir_done = [0] * 16
    headS_tp = [0] * 16
    oproj_tp = [0] * 16
    op_evac = [0] * NTT
    op_dma = [0] * NTT
    prev_pv = [0] * 8
    prev_exp7 = 0
    prev_SLR = 0
    for h in range(nheads):
        hb = h % 2
        w_m = [("sp", QT_DONE)]
        if h >= 2:
            w_m.append(("tp", headS_tp[h - 2]))
        P.add("sync", lambda e, h=h, hb=hb: e.dma_start(
            mirQ[hb][0:64, 0:8, :],
            QT_sb[64:128, 0:8, h * 128:(h + 1) * 128]),
            waits=w_m, inc="q_m", inc_k=16)
        P.add("sync", lambda e, h=h, hb=hb: e.dma_start(
            mirQ[hb][64:128, 0:8, :],
            QT_sb[0:64, 0:8, h * 128:(h + 1) * 128]),
            inc="q_m", inc_k=16)
        mir_done[h] = P.now("q_m")

        for G in range(2):
            G4 = G * 4
            BK = slice(h * 128, (h + 1) * 128)
            exp_done = [0] * 8
            pv_done = [0] * 8

            def emit_S(kc):
                w_s = []
                if kc == 0:
                    w_s = [("sp", prev_exp7)]
                    if G == 0:
                        w_s.append(("q_m", mir_done[h]))
                        if h == 0:
                            w_s += [("sp", PROJ_S), ("sp", CB_S)]
                else:
                    w_s = [("sp", exp_done[kc - 1])]
                P.add("tensor", lambda e, kc=kc, BK=BK, G4=G4: e.matmul(
                    PS[0][:, 0:512], KT_sb[0:64, kc, BK],
                    QT_sb[0:64, G4:G4 + 4, BK],
                    start=True, stop=True, tile_position=(0, 0),
                    skip_group_check=True), waits=w_s)
                P.add("tensor", lambda e, kc=kc, BK=BK, G4=G4, hb=hb: e.matmul(
                    PS[3][:, 0:512], KT_sb[64:128, kc, BK],
                    mirQ[hb][64:128, G4:G4 + 4, :],
                    start=True, stop=True, tile_position=(64, 0),
                    skip_group_check=True))
                P.add("tensor", lambda e, kc=kc, BK=BK, G4=G4, hb=hb: e.matmul(
                    PS[0][:, 512:1024], KT_sb[0:64, kc, BK],
                    mirQ[hb][0:64, G4:G4 + 4, :],
                    start=True, stop=True, tile_position=(0, 0),
                    skip_group_check=True))
                P.add("tensor", lambda e, kc=kc, BK=BK, G4=G4: e.matmul(
                    PS[3][:, 512:1024], KT_sb[64:128, kc, BK],
                    QT_sb[64:128, G4:G4 + 4, BK],
                    start=True, stop=True, tile_position=(64, 0),
                    skip_group_check=True), inc="tp")
                TS = P.now("tp")
                w_e = [("tp", TS)]
                if kc >= 2:
                    w_e.append(("tp", pv_done[kc - 2]))
                else:
                    w_e.append(("tp", prev_pv[kc + 6]))
                par = kc % 2
                P.add("scalar", lambda e, par=par, h=h: e.activation(
                    P_A[par][:, :], PS[0][:, 0:1024], AF.Exp,
                    bias=biasrow[:, h:h + 1]), waits=w_e)
                exp_done[kc] = P.add("scalar", lambda e, par=par, h=h: e.activation(
                    P_B[par][:, :], PS[3][:, 0:1024], AF.Exp,
                    bias=biasrow[:, h:h + 1]), inc="sp")

            def emit_PV(kc):
                par = kc % 2
                w_p = [("sp", exp_done[kc])]
                if kc == 0:
                    w_p.append(("sp", prev_SLR))
                    if h == 0 and G == 0:
                        w_p.append(("g_i", GI_V))
                for sl in range(2):
                    P.add("tensor", lambda e, kc=kc, sl=sl, par=par, h=h: e.matmul(
                        PS[2][0:65, sl * 512:(sl + 1) * 512],
                        Vtm[:, h, (2 * kc) * 65:(2 * kc) * 65 + 65],
                        P_A[par][:, sl * 512:(sl + 1) * 512],
                        start=(kc == 0), stop=False,
                        tile_position=(0, 0), skip_group_check=True),
                        waits=w_p if sl == 0 else [])
                    P.add("tensor", lambda e, kc=kc, sl=sl, par=par, h=h: e.matmul(
                        PS[2][0:65, sl * 512:(sl + 1) * 512],
                        Vtm[:, h, (2 * kc + 1) * 65:(2 * kc + 1) * 65 + 65],
                        P_B[par][:, sl * 512:(sl + 1) * 512],
                        start=False, stop=(kc == 7),
                        tile_position=(0, 0), skip_group_check=True),
                        inc="tp" if sl == 1 else None)
                pv_done[kc] = P.now("tp")

            for kc in range(8):
                emit_S(kc)
                if kc >= 1:
                    emit_PV(kc - 1)
            emit_PV(7)
            if debug and h == 0 and G == 1:
                tapnow("P_A0", P_A[1][:, :], [("sp", exp_done[1])])
                tapnow("P_B0", P_B[1][:, :], [("sp", exp_done[1])])

            SAR = P.add("scalar", lambda e: e.copy(
                araw[0:64, :], PS[2][0:64, 0:1024]),
                waits=[("tp", pv_done[7])], inc="sp")
            VREC = P.add("vector", lambda e: e.reciprocal(
                lrecf[64:65, :], PS[2][64:65, 0:1024]),
                waits=[("tp", pv_done[7]), ("g_i", GI_L)], inc="vp")
            SLB = P.add("scalar", lambda e: e.copy(
                lABb[0:1, :], lrecf[64:65, :]), waits=[("vp", VREC)], inc="sp")
            prev_SLR = SLB
            w_lt = [("sp", SLB)]
            if G == 0 and h > 0:
                w_lt.append(("sp", op_evac[h - 1]))
            for qt in range(8):
                w_t = w_lt if qt == 0 else []
                P.add("tensor", lambda e, qt=qt: e.transpose(
                    PSb[1][:, qt * 64:qt * 64 + 33],
                    lABb[0:33, qt * 128:(qt + 1) * 128], ident_s[0:33, 0:33]),
                    waits=w_t, inc="tp" if qt == 7 else None)
            TLT = P.now("tp")
            VRT = P.add("vector", lambda e: e.tensor_copy(
                rT_f[:, 0:8],
                PSb[1][:, 0:512].rearrange("p (q e) -> p q e", e=64)[:, :, 0:1].squeeze(2)),
                waits=[("tp", TLT)], inc="vp")
            for j in range(8):
                w_t = [("sp", SAR)] if j == 0 else []
                P.add("tensor", lambda e, j=j: e.transpose(
                    PSb[1][:, 1024 + j * 64:1024 + (j + 1) * 64],
                    araw[0:64, j * 128:(j + 1) * 128], ident_s[0:64, 0:64]),
                    waits=w_t, inc="tp" if j == 7 else None)
            TAT = P.now("tp")
            for j in range(8):
                w_v = [("tp", TAT), ("vp", VRT)] if j == 0 else []
                P.add("vector", lambda e, j=j: e.tensor_scalar(
                    tnorm[:, j * 64:(j + 1) * 64],
                    PSb[1][:, 1024 + j * 64:1024 + (j + 1) * 64],
                    rT_f[:, j:j + 1], None, ALU.mult),
                    waits=w_v, inc="vp" if j == 7 else None)
            VTN = P.now("vp")
            for ocj in range(4):
                c0 = 256 + ocj * 128 if ocj < 2 else 512 + ocj * 128
                w_t = [("vp", VTN)] if ocj == 0 else []
                P.add("tensor", lambda e, ocj=ocj, c0=c0: e.matmul(
                    PS[1][0:64, c0:c0 + 128],
                    tnorm[:, ocj * 64:(ocj + 1) * 64], ident_s[:, :],
                    start=True, stop=True, tile_position=(0, 0),
                    skip_group_check=True), waits=w_t)
                P.add("tensor", lambda e, ocj=ocj, c0=c0: e.matmul(
                    PS[1][64:128, c0:c0 + 128],
                    tnorm[:, 256 + ocj * 64:256 + (ocj + 1) * 64], ident_s[:, :],
                    start=True, stop=True, tile_position=(0, 64),
                    skip_group_check=True),
                    inc="tp" if ocj == 3 else None)
            TNB = P.now("tp")
            w_a = [("tp", TNB)]
            if G == 0 and h >= 1:
                w_a.append(("g_o", op_dma[h - 1]))
                if h >= 2:
                    w_a.append(("tp", oproj_tp[h - 2]))
            P.add("scalar", lambda e, hb=hb, G4=G4: e.copy(
                aT_h[hb][:, G4:G4 + 2, :],
                PS[1][:, 256:512].rearrange("p (j q) -> p j q", q=128)),
                waits=w_a)
            tail_sp = P.add("scalar", lambda e, hb=hb, G4=G4: e.copy(
                aT_h[hb][:, G4 + 2:G4 + 4, :],
                PS[1][:, 768:1024].rearrange("p (j q) -> p j q", q=128)),
                inc="sp")
            prev_pv = pv_done
            prev_exp7 = exp_done[7]
            if debug and h == 0 and G == 1:
                tapnow("araw0", araw[0:64, :], [("sp", tail_sp)])
                tapnow("rT0", rT_f[:, 0:8], [("vp", VRT)])
                tapnow("tnorm0", tnorm[:, :], [("sp", tail_sp)])
        headS_tp[h] = pv_done[7]
        for hh in range(2):
            for oc in range(8):
                w_m2 = []
                if hh == 0 and oc == 0:
                    w_m2 = [("sp", tail_sp), ("gw", GW_WO)]
                P.add("tensor", lambda e, hb=hb, hh=hh, oc=oc: e.matmul(
                    PS[1][:, hh * 512:(hh + 1) * 512],
                    aT_h[hb][:, oc, :],
                    woT_sb[:, oc, hh * 512:(hh + 1) * 512],
                    start=(oc == 0), stop=(oc == 7)),
                    waits=w_m2, inc="tp" if (hh == 1 and oc == 7) else None)
        oproj_tp[h] = P.now("tp")
        op_evac[h] = P.add("scalar", lambda e, hb=hb: e.copy(
            aT_h[1 - hb][:, 0:8, :],
            PS[1][:, 0:1024].rearrange("p (j q) -> p j q", q=128)),
            waits=[("tp", oproj_tp[h])], inc="sp")
        op_dma[h] = P.add("gpsimd", lambda e, h=h, hb=hb: e.dma_start(
            ao_hbm[h * 128:(h + 1) * 128, :].rearrange("p (j q) -> p j q", q=128),
            aT_h[1 - hb][:, 0:8, :]),
            waits=[("sp", op_evac[h])], inc="g_o", inc_k=16)
        if debug and h == 0:
            tapnow("aTh0", aT_h[0][:, 0:8, :], [("sp", op_evac[0])])
    B2_S = op_evac[nheads - 1]
    B2_T = P.now("tp")
    if upto == "attn":
        return _early_out("attn")

    P.add("sync", lambda e: e.dma_start(xqT[:, 0:8, :], xqT_hbm[:, 0:8, :]),
          waits=[("sp", B2_S)], inc="q_f", inc_k=16)
    P.add("sync", lambda e: e.dma_start(xqlT[:, 0:8, :], xql_hbm[:, 0:8, :]),
          inc="q_f", inc_k=16)

    chunk_ld = [0] * len(CHUNKS)
    chunk_td = [0] * len(CHUNKS)
    acc_vp = [[0] * NTT for _ in range(len(CHUNKS))]
    _chunks = CHUNKS if nchunks is None else CHUNKS[:nchunks]
    chunk_lg = [0] * len(CHUNKS)
    for c, csz in enumerate(_chunks):
        b = c % 2
        off = CHUNK_OFF[c]
        nj = csz // 128
        w_l = [("gw", B1_G)]
        if c <= 1:
            w_l.append(("sp", B2_S))
            w_l.append(("tp", B2_T))
        if c >= 1:
            w_l.append(("tp", chunk_td[c - 1]))
        P.add("sync", lambda e, off=off, csz=csz: e.dma_start(
            wgq_sb[:, 0:8, 0:csz], wgq_hbm[:, 0:8, off:off + csz]),
            waits=w_l, inc="q_f", inc_k=16)
        P.add("sync", lambda e, off=off, csz=csz: e.dma_start(
            wgql_sb[:, 0:8, 0:csz], wgql_hbm[:, 0:8, off:off + csz]),
            inc="q_f", inc_k=16)
        w_d = []
        if c >= 2:
            w_d.append(("tp", chunk_td[c - 2]))
        P.add("sync", lambda e, b=b, off=off, nj=nj: e.dma_start(
            wdq_sb[b][:, 0:nj, :],
            wdq_hbm[off // 128:off // 128 + nj, :, :].transpose([1, 0, 2])),
            waits=w_d, inc="q_f", inc_k=16)
        chunk_ld[c] = P.now("q_f")
        P.add("gpsimd", lambda e, off=off, csz=csz: e.dma_start(
            wuq_sb[:, 0:8, 0:csz], wuq_hbm[:, 0:8, off:off + csz]),
            waits=w_l, inc="gw", inc_k=16)
        P.add("gpsimd", lambda e, off=off, csz=csz: e.dma_start(
            wuql_sb[:, 0:8, 0:csz], wuql_hbm[:, 0:8, off:off + csz]),
            inc="gw", inc_k=16)
        chunk_lg[c] = P.now("gw")
        for tt in range(NTT):
            for hc in range(8):
                for ps in range(3):
                    w_m = []
                    if hc == 0 and ps == 0:
                        w_m = [("q_f", chunk_ld[c]), ("gw", chunk_lg[c])]
                        w_m.append(("vp", P.now("vp")))
                    xop = xqlT if ps == 2 else xqT
                    wop = wgql_sb if ps == 1 else wgq_sb
                    P.add("tensor", lambda e, hc=hc, tt=tt, csz=csz, xop=xop, wop=wop, ps=ps: e.matmul(
                        PS[0][:, 0:csz], xop[:, hc, tt * 128:(tt + 1) * 128],
                        wop[:, hc, 0:csz],
                        start=(hc == 0 and ps == 0), stop=(hc == 7 and ps == 2)),
                        waits=w_m, inc="tp" if (hc == 7 and ps == 2) else None)
            TG = P.now("tp")
            SSI = P.add("scalar", lambda e, csz=csz: e.activation(
                sg_sb[:, 0:csz], PS[0][:, 0:csz], AF.Silu),
                waits=[("tp", TG)], inc="sp")
            for hc in range(8):
                for ps in range(3):
                    xop = xqlT if ps == 2 else xqT
                    wop = wuql_sb if ps == 1 else wuq_sb
                    P.add("tensor", lambda e, hc=hc, tt=tt, csz=csz, xop=xop, wop=wop, ps=ps: e.matmul(
                        PS[0][:, 512:512 + csz], xop[:, hc, tt * 128:(tt + 1) * 128],
                        wop[:, hc, 0:csz],
                        start=(hc == 0 and ps == 0), stop=(hc == 7 and ps == 2)),
                        inc="tp" if (hc == 7 and ps == 2) else None)
            TU = P.now("tp")
            VH = P.add("vector", lambda e, csz=csz: e.tensor_tensor(
                h_sb[:, 0:csz], sg_sb[:, 0:csz], PS[0][:, 512:512 + csz],
                ALU.mult), waits=[("sp", SSI), ("tp", TU)], inc="vp")
            VHQ = qd(h_sb, hq_sb[:, 0:512], hM, hS, hR, hT_sb, hA_sb, hQ_sb,
                     512, waits=[])
            for j in range(nj):
                w_t = []
                if j == 0:
                    w_t = [("vp", VHQ), ("sp", P.now("sp"))]
                P.add("tensor", lambda e, j=j: e.transpose(
                    PSb[2][:, j * 128:(j + 1) * 128],
                    hq_sb[:, j * 128:(j + 1) * 128], ident_s[:, :]),
                    waits=w_t, inc="tp" if j == nj - 1 else None)
            THT = P.now("tp")
            SHE = P.add("scalar", lambda e, csz=csz: e.copy(
                hqT_sb[:, 0:csz], PSb[2][:, 0:csz]),
                waits=[("tp", THT)], inc="sp")
            if c == 0 and tt == 0:
                tapnow("wgq0", wgq_sb[:, 0:8, 0:512], [("q_f", chunk_ld[0])])
                tapnow("wuq0", wuq_sb[:, 0:8, 0:512], [("gw", chunk_lg[0])])
                tapnow("wdq0", wdq_sb[0][:, 0:4, :], [("q_f", chunk_ld[0])])
                tapnow("sg0", sg_sb[:, :], [("sp", SSI)])
                tapnow("h0", h_sb[:, :], [("vp", VH)])
                tapnow("hq0", hq_sb[:, :], [("vp", VHQ)])
                tapnow("hqT0", hqT_sb[:, :], [("sp", SHE)])
            for hh in range(2):
                for j in range(nj):
                    w_m = []
                    if hh == 0 and j == 0:
                        w_m = [("sp", SHE)]
                        w_m.append(("vp", P.now("vp")))
                    P.add("tensor", lambda e, hh=hh, j=j, b=b, nj=nj: e.matmul(
                        PS[1][:, hh * 512:(hh + 1) * 512],
                        hqT_sb[:, j * 128:(j + 1) * 128],
                        wdq_sb[b][:, j, hh * 512:(hh + 1) * 512],
                        start=(j == 0), stop=(j == nj - 1)),
                        waits=w_m,
                        inc="tp" if (hh == 1 and j == nj - 1) else None)
            TD = P.now("tp")
            if c == 0:
                acc_vp[c][tt] = P.add("vector", lambda e, tt=tt: e.tensor_copy(
                    ffn_acc[:, tt * H:(tt + 1) * H], PS[1][:, 0:1024]),
                    waits=[("tp", TD)], inc="vp")
            else:
                acc_vp[c][tt] = P.add("vector", lambda e, tt=tt: e.tensor_tensor(
                    ffn_acc[:, tt * H:(tt + 1) * H],
                    ffn_acc[:, tt * H:(tt + 1) * H], PS[1][:, 0:1024],
                    ALU.add), waits=[("tp", TD)], inc="vp")
            chunk_td[c] = P.now("tp")
    NCH = len(_chunks)
    tapnow("ffn0", ffn_acc[:, 0:1024], [("vp", acc_vp[NCH - 1][0])])
    if upto == "b3":
        return _early_out("b3")

    B3_V = P.now("vp")
    vo_hist = [0] * NTT
    st_hist = [0] * NTT
    for tt in range(NTT):
        b = tt % 2
        w = [("vp", B3_V)] if tt < 2 else [("vp", vo_hist[tt - 2]),
                                           ("g_st", st_hist[tt - 2])]
        P.add("sync", lambda e, tt=tt, b=b: e.dma_start(
            xr_t[b][:, :], x_in[tt * 128:(tt + 1) * 128, :]),
            waits=w, inc="q_r", inc_k=16)
        P.add("sync", lambda e, tt=tt, b=b: e.dma_start(
            ao_t[b][:, :], ao_hbm[tt * 128:(tt + 1) * 128, :]),
            waits=[("g_o", op_dma[tt])], inc="q_r", inc_k=16)
        QR = P.now("q_r")
        wv = [("q_r", QR), ("vp", acc_vp[NCH - 1][tt]), ("sp", op_evac[tt])]
        if tt >= 2:
            wv.append(("g_st", st_hist[tt - 2]))
        P.add("vector", lambda e, tt=tt, b=b: e.tensor_tensor(
            ob_t[b][:, :], ffn_acc[:, tt * H:(tt + 1) * H], xr_t[b][:, :],
            ALU.add), waits=wv)
        VO = P.add("vector", lambda e, tt=tt, b=b: e.tensor_tensor(
            ob_t[b][:, :], ob_t[b][:, :], ao_t[b][:, :],
            ALU.add), inc="vp")
        vo_hist[tt] = VO
        st_hist[tt] = P.add("gpsimd", lambda e, tt=tt, b=b: e.dma_start(
            y_out[tt * 128:(tt + 1) * 128, :], ob_t[b][:, :]),
            waits=[("vp", VO)], inc="g_st", inc_k=16)
    P.add("gpsimd", lambda e: e.wait_ge(P.sems["g_st"], NTT * 16))

    P.emit()
    nc.compile()
    return nc


_NC_CACHE = {}


def _get_nc():
    if "nc" not in _NC_CACHE:
        _NC_CACHE["nc"] = build()
    return _NC_CACHE["nc"]


def kernel(L0_attn_x, L0_attn_wq, L0_attn_wk, L0_attn_wv, L0_attn_wo,
           L0_ffn_w_gate, L0_ffn_w_up, L0_ffn_w_down):
    nc = _get_nc()
    x = np.ascontiguousarray(np.asarray(L0_attn_x, dtype=np.float32))
    wqT = np.ascontiguousarray(np.asarray(L0_attn_wq, dtype=np.float32).T)
    wkT = np.ascontiguousarray(np.asarray(L0_attn_wk, dtype=np.float32).T)
    wvT = np.ascontiguousarray(np.asarray(L0_attn_wv, dtype=np.float32).T)
    woT = np.ascontiguousarray(np.asarray(L0_attn_wo, dtype=np.float32).T)
    wg = np.ascontiguousarray(np.asarray(L0_ffn_w_gate, dtype=np.float32))
    wu = np.ascontiguousarray(np.asarray(L0_ffn_w_up, dtype=np.float32))
    wd = np.ascontiguousarray(np.asarray(L0_ffn_w_down, dtype=np.float32))
    ident = np.eye(128, dtype=ml_dtypes.bfloat16)
    in_maps = []
    for c in range(8):
        in_maps.append(dict(
            x_in=np.ascontiguousarray(x[c]),
            wqT=wqT, wkT=wkT, wvT=wvT, woT=woT,
            wg=wg, wu=wu, wd=wd, ident=ident,
        ))
    res = run_bass_kernel_spmd(nc, in_maps, core_ids=list(range(8)))
    out = np.stack([np.asarray(res.results[c]["y"], dtype=np.float32)
                    for c in range(8)], axis=0)
    return out


# version 11 (final)
# speedup vs baseline: 1.2180x; 1.2180x over previous
import numpy as np
import ml_dtypes

import concourse.bass as bass
import concourse.mybir as mybir
from concourse.bass_utils import run_bass_kernel_spmd
from concourse.bacc import Bacc

F32 = mybir.dt.float32
BF16 = mybir.dt.bfloat16
I32 = mybir.dt.int32
AF = mybir.ActivationFunctionType
ALU = mybir.AluOpType
AX = mybir.AxisListType

from concourse.dve_op import DveOp
from concourse.dve_ops import (
    OPS as _DVE_OPS,
    CUSTOM_DVE_SPECS as _DVE_SPECS,
    CUSTOM_DVE_REFS as _DVE_REFS,
    _SUB_OPCODE_FOR_NAME as _DVE_ROWS,
    _CUSTOM_DVE_ROW_BASE as _DVE_ROW_BASE,
)
from concourse.dve_spec import (
    Spec, Src0, C0, C1, C2, One,
    lower as _dve_lower, AluOp as _DAlu, Bin as _DBin,
)

_GRID = np.array([0.0, 0.5, 1.0, 1.5, 2.0, 3.0, 4.0, 6.0])
_MID = np.array([0.25, 0.75, 1.25, 1.75, 2.5, 3.5, 5.0])
HC = float(np.float32(0.25) - np.float32(2.0 ** -23))
DENORM1 = float(np.frombuffer(np.uint32(0x003FFFFF).tobytes(), dtype=np.float32)[0])
DENORM2 = float(np.frombuffer(np.uint32(0x007FFFFF).tobytes(), dtype=np.float32)[0])
SGN_I32 = -2147483648


def _fp4_round_ref(in0, in1, s0, s1, imm2):
    a = np.abs(in0)
    idx = np.searchsorted(_MID, a, side="left")
    g = _GRID[idx]
    return g.astype(np.float64 if in0.dtype == np.float64 else np.float32)


def _make_fp4_op():
    c = Src0 < One
    v = Src0 + c
    e2 = _DBin(_DAlu.BITWISE_AND, v, _DBin(_DAlu.BITWISE_NOT, C1, C1))
    h = e2 * C2
    w = v + h
    g = _DBin(_DAlu.BITWISE_AND, w, _DBin(_DAlu.BITWISE_NOT, C0, C0))
    q = g - c
    spec = Spec(body=q)
    shas = {v_: _dve_lower(spec, ver=v_, name="FP4_ROUND_ANT").sha(v_)
            for v_ in ("v3", "v4")}
    return DveOp("FP4_ROUND_ANT", spec, subdim=False, uops_sha=shas,
                 reference=_fp4_round_ref)


if "FP4_ROUND_ANT" not in _DVE_ROWS:
    FP4_ROUND = _make_fp4_op()
    _DVE_OPS.append(FP4_ROUND)
    _DVE_SPECS[FP4_ROUND.name] = FP4_ROUND.spec
    _DVE_REFS[FP4_ROUND.name] = FP4_ROUND.ref
    _DVE_ROWS[FP4_ROUND.name] = _DVE_ROW_BASE + len(_DVE_OPS) - 1
else:
    FP4_ROUND = next(o for o in _DVE_OPS if o.name == "FP4_ROUND_ANT")

T = 2048
H = 1024
II = 2816
NPAIR = 8
NTT = 16
KC = 16
CHUNKS = [512] * 5 + [256]
CHUNK_OFF = [0, 512, 1024, 1536, 2048, 2560]


class Prog:
    ENGINES = ("sync", "gpsimd", "vector", "scalar", "tensor")

    def __init__(self, nc):
        self.nc = nc
        self.entries = []
        self.sems = {}
        self.cnt = {}

    def sem(self, name):
        if name not in self.sems:
            self.sems[name] = self.nc.alloc_semaphore(name)
            self.cnt[name] = 0
        return self.sems[name]

    def add(self, eng, fn, waits=(), inc=None, inc_k=1):
        waits = [(s, v) for (s, v) in waits if v > 0]
        for (s, _v) in waits:
            self.sem(s)
        self.entries.append((eng, fn, waits, inc, inc_k))
        if inc is not None:
            self.sem(inc)
            self.cnt[inc] += inc_k
            return self.cnt[inc]
        return None

    def now(self, name):
        self.sem(name)
        return self.cnt[name]

    def emit(self):
        with self.nc.Block() as block:
            for engname in self.ENGINES:
                entries = [e for e in self.entries if e[0] == engname]
                if not entries:
                    continue

                def section(eng, entries=entries):
                    last = {}
                    for (_, fn, waits, inc, inc_k) in entries:
                        for (semname, val) in waits:
                            if last.get(semname, 0) >= val:
                                continue
                            eng.wait_ge(self.sems[semname], val)
                            last[semname] = val
                        ins = fn(eng)
                        if inc is not None:
                            ins.then_inc(self.sems[inc], inc_k)
                getattr(block, engname)(section)


def build(debug=False, upto="all", nheads=16, nchunks=None):
    nc = Bacc("TRN2", target_bir_lowering=False, debug=False,
              detect_race_conditions=False)

    x_in = nc.dram_tensor("x_in", [T, H], F32, kind="ExternalInput")
    wqT_h = nc.dram_tensor("wqT", [H, H], F32, kind="ExternalInput")
    wkT_h = nc.dram_tensor("wkT", [H, H], F32, kind="ExternalInput")
    wvT_h = nc.dram_tensor("wvT", [H, H], F32, kind="ExternalInput")
    woT_h = nc.dram_tensor("woT", [H, H], F32, kind="ExternalInput")
    wg_h = nc.dram_tensor("wg", [II, H], F32, kind="ExternalInput")
    wu_h = nc.dram_tensor("wu", [II, H], F32, kind="ExternalInput")
    wd_h = nc.dram_tensor("wd", [H, II], F32, kind="ExternalInput")
    ident_h = nc.dram_tensor("ident", [128, 128], BF16, kind="ExternalInput")
    y_out = nc.dram_tensor("y", [T, H], F32, kind="ExternalOutput")
    wgq_hbm = nc.dram_tensor("wgq_s", [128, 8, II], BF16)
    wuq_hbm = nc.dram_tensor("wuq_s", [128, 8, II], BF16)
    wdq_hbm = nc.dram_tensor("wdq_s", [22, 128, H], BF16)
    xqT_hbm = nc.dram_tensor("xqT_s", [128, 8, T], BF16)
    xql_hbm = nc.dram_tensor("xql_s", [128, 8, T], BF16)
    wgql_hbm = nc.dram_tensor("wgql_s", [128, 8, II], BF16)
    wuql_hbm = nc.dram_tensor("wuql_s", [128, 8, II], BF16)
    ao_hbm = nc.dram_tensor("ao_s", [T, H], BF16)

    def sb(name, shape, dtype, off):
        return nc.alloc_sbuf_tensor_at(name, shape, dtype, offset=off)

    F2O = 18432
    AO = 34816
    BO = 67584
    CO = 100352
    DO = 133120
    EO = 198656
    HO = 210944

    xnT = sb("xnT", [128, 8, T], BF16, AO)
    xqlT = sb("xqlT", [128, 8, T], BF16, AO)
    xqT = sb("xqT", [128, 8, T], BF16, BO)
    QT_sb = sb("QT_sb", [128, 8, T], BF16, BO)
    KT_sb = sb("KT_sb", [128, 8, T], BF16, CO)
    wgq_sb = sb("wgq", [128, 8, 512], BF16, CO)
    wgql_sb = sb("wgql", [128, 8, 512], BF16, CO + 8192)
    wuq_sb = sb("wuq", [128, 8, 512], BF16, CO + 16384)
    wuql_sb = sb("wuql", [128, 8, 512], BF16, CO + 24576)
    ob_t = [sb(f"ob_t{b}", [128, H], F32, CO + b * 4096) for b in range(2)]
    ao_t = [sb(f"ao_t{b}", [128, H], BF16, EO + 8192 + b * 2048) for b in range(2)]
    Vtm = sb("Vtm", [128, 16, 1040], BF16, DO)
    wb2 = sb("wb2", [128, 8, H], BF16, DO + 33280)
    woT_sb = sb("woT_sb", [128, 8, H], BF16, DO + 33280)
    ffn_acc = sb("ffn_acc", [128, NTT * H], F32, DO)
    mirQ = [sb(f"mirQ{b}", [128, 8, 128], BF16, EO + b * 2048) for b in range(2)]
    aT_h = [sb(f"aTh{b}", [128, 8, 128], BF16, EO + 4096 + b * 2048)
            for b in range(2)]
    araw = sb("araw", [64, 1024], BF16, EO + 8192)
    Vf = sb("Vf", [128, H], BF16, EO + 8192)
    tnorm = sb("tnorm", [128, 512], BF16, EO + 10240)
    P_A = [sb(f"P_A{b}", [128, 1024], BF16, F2O + b * 2048) for b in range(2)]
    P_B = [sb(f"P_B{b}", [128, 1024], BF16, F2O + 4096 + b * 2048) for b in range(2)]
    Qsqbuf = sb("Qsqbuf", [128, T], BF16, F2O)
    xsq_f = sb("xsq_f", [128, H], F32, DO)
    wA_f = sb("wA_f", [128, 1408], F32, DO)
    x_t = [sb(f"x_t{b}", [128, H], F32, EO + b * 4096) for b in range(2)]
    xn_f = sb("xn_f", [128, H], F32, EO + 8192)
    wM = sb("wM", [128, 96], F32, EO)
    wS = sb("wS", [128, 96], F32, EO + 512)
    wR = sb("wR", [128, 96], F32, EO + 1024)
    wq_b16 = sb("wq_b16", [128, 1408], BF16, EO + 1536)
    wstage = sb("wstage", [128, 11, 128], BF16, EO + 4352)
    sg_sb = sb("sg_sb", [128, 512], F32, EO)
    h_sb = sb("h_sb", [128, 512], F32, EO + 2048)
    hq_sb = sb("hq_sb", [128, 512], BF16, EO + 4096)
    hqT_sb = sb("hqT_sb", [128, 512], BF16, EO + 5120)
    hT_sb = sb("hT_sb", [128, 512], F32, EO + 6144)
    hA_sb = sb("hA_sb", [128, 512], F32, EO + 8192)
    hQ_sb = sb("hQ_sb", [128, 512], BF16, EO + 10240)
    hM = sb("hM", [128, 32], F32, EO + 11264)
    hS = sb("hS", [128, 32], F32, EO + 11392)
    hR = sb("hR", [128, 32], F32, EO + 11520)
    xr_t = [sb(f"xr_t{b}", [128, H], F32, EO + b * 4096) for b in range(2)]
    b0s1 = sb("b0s1", [128, 1], F32, F2O)
    b0s2 = sb("b0s2", [128, 1], F32, F2O + 32)
    b0m = sb("b0m", [128, 1], F32, F2O + 64)
    b0msq = sb("b0msq", [128, 1], F32, F2O + 96)
    b0var = sb("b0var", [128, 1], F32, F2O + 128)
    b0ln = sb("b0ln", [128, 1], F32, F2O + 160)
    b0rstd = sb("b0rstd", [128, 1], F32, F2O + 192)
    b0nmr = sb("b0nmr", [128, 1], F32, F2O + 224)
    b0M = sb("b0M", [128, 64], F32, F2O + 256)
    b0S = sb("b0S", [128, 64], F32, F2O + 512)
    b0R = sb("b0R", [128, 64], F32, F2O + 768)
    b0T = sb("b0T", [128, H], F32, F2O + 1024)
    b0A = sb("b0A", [128, H], F32, F2O + 5120)
    b0Q = sb("b0Q", [128, H], BF16, F2O + 9216)
    xn_b = sb("xn_b", [128, H], BF16, F2O + 11264)
    xq_b = sb("xq_b", [128, H], BF16, F2O + 13312)
    lABb = sb("lABb", [33, 1024], BF16, F2O + 8192)
    lrecf = sb("lrecf", [65, 1024], F32, F2O + 10240)
    mqs = sb("mqs", [34, 16, 16], F32, F2O + 14336)
    mqMX = sb("mqMX", [34, 16], F32, F2O + 15360)
    mqMXb = sb("mqMXb", [34, 16], BF16, F2O + 15424)
    cpr = sb("cpr", [1, 16], F32, F2O + 15456)
    cpos = sb("cpos", [1, 16], BF16, F2O + 15520)
    biasrow = sb("biasrow", [128, 16], F32, F2O + 15552)
    rT_f = sb("rT_f", [128, 8], F32, F2O + 15616)
    mqMX2 = sb("mqMX2", [34, 16], F32, F2O + 15648)
    wbuf = [sb(f"wbuf{b}", [128, 1408], F32, CO + b * 5632) for b in range(2)]
    wT_f = sb("wT_f", [128, 1408], F32, CO + 11264)
    wQ_b = sb("wQ_b", [128, 1408], BF16, CO + 16896)
    wdq_sb = [sb(f"wdq{b}", [128, 4, H], BF16, F2O + b * 8192) for b in range(2)]
    ident_s = sb("ident_s", [128, 128], BF16, HO)
    onesneg = sb("onesneg", [33, 128], BF16, HO + 256)
    ones1 = sb("ones1", [128, 1], BF16, HO + 512)
    bo = sb("bo", [128, 33], BF16, HO + 544)
    sgnm = sb("sgnm", [128, 1], I32, HO + 640)
    bo2 = sb("bo2", [128, 34], BF16, HO + 672)

    PS = [nc.alloc_psum_tensor(f"PS{i}", [128, 1024], F32) for i in range(4)]
    PSb = [p[:, :].bitcast(BF16) for p in PS]

    P = Prog(nc)

    _taps = []
    def tapnow(name, ap, waits):
        if not debug:
            return
        sh = list(ap.shape)
        t = nc.dram_tensor(f"dbg_{name}", sh, ap.dtype, kind="ExternalOutput")
        P.add("gpsimd", lambda e: e.dma_start(t[tuple(slice(0, s) for s in sh)], ap),
              waits=list(waits), inc="gdbg", inc_k=16)
        _taps.append(name)

    P.add("sync", lambda e: e.dma_start(ident_s[:, :], ident_h[:, :]),
          inc="q_c", inc_k=16)
    for fn in (lambda e: e.memset(onesneg[:, :], -1.0),
               lambda e: e.memset(ones1[:, :], 1.0),
               lambda e: e.memset(bo[:, :], 0.0),
               lambda e: e.memset(bo[0:64, 0:1], 1.0),
               lambda e: e.memset(bo[64:128, 32:33], 1.0),
               lambda e: e.memset(bo2[:, :], 0.0),
               lambda e: e.memset(bo2[0:64, 1:2], 1.0),
               lambda e: e.memset(bo2[64:128, 33:34], 1.0),
               lambda e: e.memset(sgnm[:, :], SGN_I32)):
        P.add("gpsimd", fn, inc="g_i")
    GI = P.now("g_i")
    QC = P.now("q_c")

    def qd(src, dst, Mb, Sb, Rb, Tb, Ab, Qb, F, waits, waits_last=()):
        nb = F // 16
        VM_ = P.add("vector", lambda e: e.tensor_reduce(
            Mb[:, 0:nb].unsqueeze(2),
            src[:, 0:F].rearrange("p (s b) -> p s b", b=16),
            AX.X, ALU.max, apply_absolute_value=True), waits=waits, inc="vp")
        SS_ = P.add("scalar", lambda e: e.activation(
            Sb[:, 0:nb], Mb[:, 0:nb], AF.Copy, scale=1.0 / 6.0, bias=1e-12),
            waits=[("vp", VM_)], inc="sp")
        P.add("vector", lambda e: e.reciprocal(Rb[:, 0:nb], Sb[:, 0:nb]),
              waits=[("sp", SS_)])
        P.add("vector", lambda e: e.tensor_tensor(
            Tb[:, 0:F].rearrange("p (s b) -> p s b", b=16),
            src[:, 0:F].rearrange("p (s b) -> p s b", b=16),
            Rb[:, 0:nb].unsqueeze(2).broadcast_to([128, nb, 16]), ALU.mult))
        P.add("vector", lambda e: e.tensor_single_scalar(
            Ab[:, 0:F].bitcast(I32), Tb[:, 0:F].bitcast(I32), 2147483647,
            ALU.bitwise_and))
        P.add("vector", lambda e: e._custom_dve(
            FP4_ROUND, out=Qb[:, 0:F], in0=Ab[:, 0:F],
            s0=DENORM1, s1=DENORM2, imm2=HC))
        P.add("vector", lambda e: e.tensor_tensor(
            Ab[:, 0:F].rearrange("p (s b) -> p s b", b=16),
            Qb[:, 0:F].rearrange("p (s b) -> p s b", b=16),
            Sb[:, 0:nb].unsqueeze(2).broadcast_to([128, nb, 16]), ALU.mult))
        P.add("vector", lambda e: e.scalar_tensor_tensor(
            src[:, 0:F].bitcast(I32), Tb[:, 0:F].bitcast(I32), sgnm[:, :],
            Ab[:, 0:F].bitcast(I32), ALU.bitwise_and, ALU.bitwise_or))
        return P.add("vector", lambda e: e.tensor_copy(dst, src[:, 0:F]),
                     waits=list(waits_last), inc="vp")


    def _early_out(tag):
        P.add("gpsimd", lambda e: e.memset(ob_t[0][:, :], 0.0),
              waits=[("sp", P.now("sp")), ("vp", P.now("vp")),
                     ("tp", P.now("tp")), ("gw", P.now("gw"))], inc="g_i")
        GE = P.now("g_i")
        for tt in range(NTT):
            P.add("gpsimd", lambda e, tt=tt: e.dma_start(
                y_out[tt * 128:(tt + 1) * 128, :], ob_t[0][:, :]),
                waits=[("g_i", GE)], inc="g_st", inc_k=16)
        P.add("gpsimd", lambda e: e.wait_ge(P.sems["g_st"], NTT * 16))
        P.emit()
        nc.compile()
        return nc
    b0_evac = [0] * NTT
    b0_trans = [0] * NTT
    b0_xnact = [0] * NTT
    b0_srstd = [0] * NTT
    b0_loev = [0] * NTT
    b0_lodma = [0] * NTT
    for tt in range(NTT):
        b = tt % 2
        w = [("q_c", QC)]
        if tt >= 2:
            w.append(("vp", b0_xnact[tt - 2]))
        P.add("sync", lambda e, tt=tt, b=b: e.dma_start(
            x_t[b][:, :], x_in[tt * 128:(tt + 1) * 128, :]),
            waits=w, inc="q_x", inc_k=16)
        QX = P.now("q_x")
        wv0 = [("q_x", QX), ("g_i", GI)]
        if tt >= 1:
            wv0.append(("sp", b0_srstd[tt - 1]))
        VS1 = P.add("vector", lambda e, b=b: e.reduce_sum(
            b0s1[:, :], x_t[b][:, :], AX.X), waits=wv0, inc="vp")
        SM = P.add("scalar", lambda e: e.activation(
            b0m[:, :], b0s1[:, :], AF.Copy, scale=1.0 / H),
            waits=[("vp", VS1)], inc="sp")
        P.add("vector", lambda e, b=b: e.tensor_scalar(
            xsq_f[:, :], x_t[b][:, :], b0m[:, :], None, ALU.subtract),
            waits=[("sp", SM)])
        P.add("vector", lambda e: e.tensor_tensor(
            b0T[:, :], xsq_f[:, :], xsq_f[:, :], ALU.mult))
        VV = P.add("vector", lambda e: e.reduce_sum(
            b0s2[:, :], b0T[:, :], AX.X), inc="vp")
        P.add("scalar", lambda e: e.activation(
            b0var[:, :], b0s2[:, :], AF.Copy, bias=1e-5, scale=1.0 / H),
            waits=[("vp", VV)])
        P.add("scalar", lambda e: e.activation(b0ln[:, :], b0var[:, :], AF.Ln))
        SR = P.add("scalar", lambda e: e.activation(
            b0rstd[:, :], b0ln[:, :], AF.Exp, scale=-0.5), inc="sp")
        b0_srstd[tt] = SR
        SX = P.add("vector", lambda e: e.tensor_scalar(
            xn_f[:, :], xsq_f[:, :], b0rstd[:, :], None, ALU.mult),
            waits=[("sp", SR)], inc="vp")
        b0_xnact[tt] = SX
        wtr = []
        if tt >= 1:
            wtr.append(("tp", b0_trans[tt - 1]))
            wtr.append(("sp", b0_loev[tt - 1]))
            wtr.append(("gw", b0_lodma[tt - 1]))
        VC = P.add("vector", lambda e: e.tensor_copy(xn_b[:, :], xn_f[:, :]),
                   waits=wtr, inc="vp")
        VQ = qd(xn_f, xq_b[:, :], b0M, b0S, b0R, b0T, b0A, b0Q, H, waits=[])
        for hc in range(8):
            w_t = []
            if hc == 0:
                w_t = [("vp", VC), ("q_c", QC)]
                if tt >= 1:
                    w_t.append(("sp", b0_evac[tt - 1]))
            P.add("tensor", lambda e, hc=hc: e.transpose(
                PSb[0][:, hc * 128:(hc + 1) * 128],
                xn_b[:, hc * 128:(hc + 1) * 128], ident_s[:, :]),
                waits=w_t, inc="tp" if hc == 7 else None)
        for hc in range(8):
            w_t = [("vp", VQ)] if hc == 0 else []
            if hc == 0 and tt >= 1:
                w_t.append(("sp", b0_loev[tt - 1]))
            P.add("tensor", lambda e, hc=hc: e.transpose(
                PSb[1][:, hc * 128:(hc + 1) * 128],
                xq_b[:, hc * 128:(hc + 1) * 128], ident_s[:, :]),
                waits=w_t, inc="tp" if hc == 7 else None)
        b0_trans[tt] = P.now("tp")
        TP = P.now("tp")
        P.add("scalar", lambda e, tt=tt: e.copy(
            xnT[:, 0:8, tt * 128:(tt + 1) * 128],
            PSb[0][:, 0:1024].rearrange("p (h q) -> p h q", q=128)),
            waits=[("tp", TP)])
        b0_evac[tt] = P.add("scalar", lambda e, tt=tt: e.copy(
            xqT[:, 0:8, tt * 128:(tt + 1) * 128],
            PSb[1][:, 0:1024].rearrange("p (h q) -> p h q", q=128)), inc="sp")
        P.add("vector", lambda e: e.tensor_copy(b0T[:, :], xq_b[:, :]))
        P.add("vector", lambda e: e.tensor_tensor(
            b0A[:, :], xn_f[:, :], b0T[:, :], ALU.subtract))
        VLB = P.add("vector", lambda e: e.tensor_copy(xq_b[:, :], b0A[:, :]),
                    waits=[("tp", b0_trans[tt])], inc="vp")
        for hc in range(8):
            w_t = [("vp", VLB), ("sp", b0_evac[tt])] if hc == 0 else []
            P.add("tensor", lambda e, hc=hc: e.transpose(
                PSb[1][:, hc * 128:(hc + 1) * 128],
                xq_b[:, hc * 128:(hc + 1) * 128], ident_s[:, :]),
                waits=w_t, inc="tp" if hc == 7 else None)
        TLO = P.now("tp")
        b0_loev[tt] = P.add("scalar", lambda e: e.copy(
            xn_b[:, 0:1024], PSb[1][:, 0:1024]),
            waits=[("tp", TLO)], inc="sp")
        b0_lodma[tt] = P.add("gpsimd", lambda e, tt=tt: e.dma_start(
            xql_hbm[:, 0:8, tt * 128:(tt + 1) * 128],
            xn_b[:, 0:1024].rearrange("p (h q) -> p h q", q=128)),
            waits=[("sp", b0_loev[tt])], inc="gw", inc_k=16)
        if tt == 0:
            tapnow("x_t0", x_t[0][:, :], [("vp", VV)])
            tapnow("b0s1", b0s1[:, :], [("vp", VV)])
            tapnow("b0s2", b0s2[:, :], [("vp", VV)])
            tapnow("b0var", b0var[:, :], [("vp", VV)])
            tapnow("b0rstd", b0rstd[:, :], [("sp", SR)])
            tapnow("xn_f0", xn_f[:, :], [("vp", SX)])
            tapnow("xn_b0", xn_b[:, :], [("vp", VC)])
            tapnow("xq_b0", xq_b[:, :], [("vp", VQ)])
            tapnow("xnT00", xnT[:, 0:8, 0:128], [("sp", b0_evac[0])])
    B0_S = P.now("sp")
    tapnow("xnT", xnT[:, 0:8, 0:256], [("sp", B0_S)])
    tapnow("xqT", xqT[:, 0:8, 0:256], [("sp", B0_S)])
    if upto == "b0":
        return _early_out("b0")

    items = ([("g", it) for it in range(22)] + [("u", it) for it in range(22)]
             + [("d", hc, hf) for hc in range(8) for hf in range(2)])
    NIT = len(items)
    b1_vq = [0] * NIT
    b1_lo = [0] * NIT
    b1_tp = [0] * NIT
    b1_se = [0] * NIT
    b1_gw = [0] * NIT
    for ii, item in enumerate(items):
        b = ii % 2
        F = 1024 if item[0] in ("g", "u") else 1408
        nt = F // 128
        wdma = [("sp", B0_S)]
        if ii >= 2:
            wdma.append(("vp", b1_lo[ii - 2]))
        if item[0] in ("g", "u"):
            src_h = wg_h if item[0] == "g" else wu_h
            it = item[1]
            P.add("sync", lambda e, b=b, src_h=src_h, it=it: e.dma_start(
                wbuf[b][:, 0:1024], src_h[it * 128:(it + 1) * 128, :]),
                waits=wdma, inc="q_w", inc_k=16)
        else:
            hc, hf = item[1], item[2]
            P.add("sync", lambda e, b=b, hc=hc, hf=hf: e.dma_start(
                wbuf[b][:, 0:1408],
                wd_h[hc * 128:(hc + 1) * 128, hf * 1408:(hf + 1) * 1408]),
                waits=wdma, inc="q_w", inc_k=16)
        QW = P.now("q_w")
        wq_waits = [("q_w", QW)]
        if ii >= 1:
            wq_waits.append(("tp", b1_tp[ii - 1]))
        b1_vq[ii] = qd(wbuf[b], wq_b16[:, 0:F], wM, wS, wR, wT_f, wA_f, wQ_b,
                       F, waits=wq_waits)
        for j in range(min(nt, 8)):
            w_t = []
            if j == 0:
                w_t = [("vp", b1_vq[ii])]
                if ii >= 1:
                    w_t.append(("sp", b1_se[ii - 1]))
            P.add("tensor", lambda e, j=j: e.transpose(
                PSb[2][:, j * 128:(j + 1) * 128],
                wq_b16[:, j * 128:(j + 1) * 128], ident_s[:, :]),
                waits=w_t, inc="tp" if j == min(nt, 8) - 1 else None)
        TP1 = P.now("tp")
        wse1 = [("tp", TP1)]
        if ii >= 1:
            wse1.append(("gw", b1_gw[ii - 1]))
        SE1 = P.add("scalar", lambda e, nt=nt: e.copy(
            wstage[:, 0:min(nt, 8), :],
            PSb[2][:, 0:min(nt, 8) * 128].rearrange("p (j q) -> p j q", q=128)),
            waits=wse1, inc="sp")
        if nt > 8:
            for j in range(8, nt):
                w_t = [("sp", SE1)] if j == 8 else []
                P.add("tensor", lambda e, j=j: e.transpose(
                    PSb[2][:, (j - 8) * 128:(j - 7) * 128],
                    wq_b16[:, j * 128:(j + 1) * 128], ident_s[:, :]),
                    waits=w_t, inc="tp" if j == nt - 1 else None)
            TP2 = P.now("tp")
            SE2 = P.add("scalar", lambda e, nt=nt: e.copy(
                wstage[:, 8:nt, :],
                PSb[2][:, 0:(nt - 8) * 128].rearrange("p (j q) -> p j q", q=128)),
                waits=[("tp", TP2)], inc="sp")
        else:
            SE2 = SE1
        b1_tp[ii] = P.now("tp")
        b1_se[ii] = SE2
        if item[0] in ("g", "u"):
            dsth = wgq_hbm if item[0] == "g" else wuq_hbm
            it = item[1]
            GWH = P.add("gpsimd", lambda e, dsth=dsth, it=it: e.dma_start(
                dsth[:, 0:8, it * 128:(it + 1) * 128], wstage[:, 0:8, :]),
                waits=[("sp", SE2)], inc="gw", inc_k=16)
            P.add("vector", lambda e: e.tensor_copy(wT_f[:, 0:1024], wq_b16[:, 0:1024]))
            b1_lo[ii] = P.add("vector", lambda e, b=b: e.tensor_tensor(
                wA_f[:, 0:1024], wbuf[b][:, 0:1024], wT_f[:, 0:1024],
                ALU.subtract), inc="vp")
            VW3 = P.add("vector", lambda e: e.tensor_copy(
                wq_b16[:, 0:1024], wA_f[:, 0:1024]),
                waits=[("tp", b1_tp[ii])], inc="vp")
            for j in range(8):
                w_t = [("vp", VW3), ("sp", SE2)] if j == 0 else []
                P.add("tensor", lambda e, j=j: e.transpose(
                    PSb[2][:, j * 128:(j + 1) * 128],
                    wq_b16[:, j * 128:(j + 1) * 128], ident_s[:, :]),
                    waits=w_t, inc="tp" if j == 7 else None)
            b1_tp[ii] = P.now("tp")
            SE3 = P.add("scalar", lambda e: e.copy(
                wstage[:, 0:8, :],
                PSb[2][:, 0:1024].rearrange("p (j q) -> p j q", q=128)),
                waits=[("tp", b1_tp[ii]), ("gw", GWH)], inc="sp")
            b1_se[ii] = SE3
            dstl = wgql_hbm if item[0] == "g" else wuql_hbm
            b1_gw[ii] = P.add("gpsimd", lambda e, dstl=dstl, it=it: e.dma_start(
                dstl[:, 0:8, it * 128:(it + 1) * 128], wstage[:, 0:8, :]),
                waits=[("sp", SE3)], inc="gw", inc_k=16)
        else:
            b1_lo[ii] = b1_vq[ii]
            hc, hf = item[1], item[2]
            b1_gw[ii] = P.add("gpsimd", lambda e, hc=hc, hf=hf: e.dma_start(
                wdq_hbm[hf * 11:(hf + 1) * 11, :, hc * 128:(hc + 1) * 128].transpose([1, 0, 2]),
                wstage[:, 0:11, :]),
                waits=[("sp", SE2)], inc="gw", inc_k=16)
    B1_G = P.now("gw")
    B1_S = P.now("sp")
    B1_V = P.now("vp")
    B1_T = P.now("tp")
    if upto == "b1":
        return _early_out("b1")

    QSC = float(8.0 ** -0.5)

    P.add("gpsimd", lambda e: e.dma_start(xqT_hbm[:, 0:8, :], xqT[:, 0:8, :]),
          waits=[("sp", B0_S)], inc="gw", inc_k=16)
    GW_XQ = P.now("gw")
    P.add("gpsimd", lambda e: e.memset(lABb[:, :], 0.0),
          waits=[("sp", B0_S), ("vp", B1_V)], inc="g_i")
    GI_L = P.now("g_i")
    P.add("gpsimd", lambda e: e.memset(
        Vtm[:, :, :].rearrange("p s (c d) -> p s c d", d=65)[:, :, :, 64:65], 1.0),
        inc="g_i")
    GI_V = P.now("g_i")

    proj_tp = [0, 0, 0]
    QT_DONE = 0
    for m_i, wh in enumerate((wqT_h, wkT_h, wvT_h)):
        w_l = [("gw", B1_G)]
        if m_i >= 1:
            w_l.append(("tp", proj_tp[m_i - 1]))
        P.add("gpsimd", lambda e, wh=wh: e.dma_start(
            wb2[:, 0:8, :], wh[:, :].rearrange("(c p) o -> p c o", p=128)),
            waits=w_l, inc="gw", inc_k=16)
        GW_M = P.now("gw")
        first = True
        for s8 in range(8):
            for half in range(2):
                for sl in range(2):
                    for hcc in range(8):
                        w_s = []
                        if sl == 0 and hcc == 0:
                            w_s = [("sp", P.now("sp"))]
                            if first:
                                w_s += [("gw", GW_M), ("sp", B0_S), ("sp", B1_S)]
                                first = False
                        P.add("tensor", lambda e, hcc=hcc, s8=s8, half=half, sl=sl: e.matmul(
                            PS[0][:, sl * 512:(sl + 1) * 512],
                            wb2[:, hcc, s8 * 128:(s8 + 1) * 128],
                            xnT[:, hcc, (half * 2 + sl) * 512:(half * 2 + sl + 1) * 512],
                            start=(hcc == 0), stop=(hcc == 7),
                            skip_group_check=True),
                            waits=w_s, inc="tp" if (sl == 1 and hcc == 7) else None)
                TPH = P.now("tp")
                if m_i < 2:
                    dst = QT_sb if m_i == 0 else KT_sb
                    w_e = [("tp", TPH)]
                    if m_i == 0 and s8 == 0 and half == 0:
                        w_e.append(("gw", GW_XQ))
                    if m_i == 1 and s8 == 0 and half == 0:
                        w_e += [("vp", B1_V), ("tp", B1_T)]
                    P.add("scalar", lambda e, dst=dst, s8=s8, half=half: e.activation(
                        dst[:, s8, half * 1024:(half + 1) * 1024],
                        PS[0][:, 0:1024], AF.Copy, scale=QSC),
                        waits=w_e, inc="sp")
                else:
                    w_e = [("tp", TPH)]
                    if s8 == 0 and half == 0:
                        w_e.append(("vp", B1_V))
                    SVF = P.add("scalar", lambda e: e.copy(
                        Vf[:, 0:1024], PS[0][:, 0:1024]), waits=w_e, inc="sp")
                    for st in range(8):
                        w_t = [("sp", SVF)] if st == 0 else []
                        P.add("tensor", lambda e, st=st: e.transpose(
                            PSb[1][:, st * 128:(st + 1) * 128],
                            Vf[:, st * 128:(st + 1) * 128], ident_s[:, :]),
                            waits=w_t, inc="tp" if st == 7 else None)
                    TVT = P.now("tp")
                    P.add("scalar", lambda e, s8=s8, half=half: e.copy(
                        Vtm[:, half * 8:(half + 1) * 8,
                            2 * s8 * 65:2 * s8 * 65 + 64],
                        PSb[1][:, 0:1024].rearrange("p (j q) -> p j q", q=128)[:, :, 0:64]),
                        waits=[("tp", TVT)])
                    P.add("scalar", lambda e, s8=s8, half=half: e.copy(
                        Vtm[:, half * 8:(half + 1) * 8,
                            (2 * s8 + 1) * 65:(2 * s8 + 1) * 65 + 64],
                        PSb[1][:, 0:1024].rearrange("p (j q) -> p j q", q=128)[:, :, 64:128]),
                        inc="sp")
        proj_tp[m_i] = P.now("tp")
        if m_i == 0:
            QT_DONE = P.now("sp")
    PROJ_S = P.now("sp")
    if debug:
        tapnow("QT", QT_sb[:, 0:8, 0:256], [("sp", QT_DONE)])
        tapnow("KT", KT_sb[:, 0:8, 0:256], [("sp", PROJ_S)])
        tapnow("Vtm", Vtm[:, 0:2, 0:1024], [("sp", PROJ_S)])
    if upto == "proj":
        return _early_out("proj")

    P.add("gpsimd", lambda e: e.dma_start(
        woT_sb[:, 0:8, :], woT_h[:, :].rearrange("(c p) o -> p c o", p=128)),
        waits=[("tp", proj_tp[2])], inc="gw", inc_k=16)
    GW_WO = P.now("gw")

    P.add("gpsimd", lambda e: e.memset(mqs[:, :, :], 0.0),
          waits=[("sp", B0_S)], inc="g_i")
    GI_M = P.now("g_i")
    for qk, SRC, bom in ((0, QT_sb, bo), (1, KT_sb, bo2)):
        nr = 33 + qk
        for t in range(8):
            w_q = [("sp", QT_DONE if qk == 0 else PROJ_S)]
            if qk > 0 or t > 0:
                w_q.append(("tp", P.now("tp")))
            VSQ = P.add("vector", lambda e, SRC=SRC, t=t: e.tensor_tensor(
                Qsqbuf[:, :], SRC[:, t, :], SRC[:, t, :], ALU.mult),
                waits=w_q, inc="vp")
            for r in range(2):
                for sl in range(2):
                    w_cm = []
                    if sl == 0:
                        w_cm = [("vp", VSQ if r == 0 else P.now("vp"))]
                    P.add("tensor", lambda e, r=r, sl=sl, bom=bom, nr=nr: e.matmul(
                        PS[3][0:nr, sl * 512:(sl + 1) * 512], bom[:, 0:nr],
                        Qsqbuf[:, r * 1024 + sl * 512:r * 1024 + (sl + 1) * 512],
                        start=True, stop=True, skip_group_check=True),
                        waits=w_cm, inc="tp" if sl == 1 else None)
                TCM = P.now("tp")
                w_r = [("tp", TCM)]
                if qk == 0 and t == 0 and r == 0:
                    w_r.append(("g_i", GI_M))
                P.add("vector", lambda e, r=r, t=t, qk=qk, nr=nr: e.tensor_reduce(
                    mqs[0:nr, r * 8:(r + 1) * 8, qk * 8 + t:qk * 8 + t + 1],
                    PS[3][0:nr, 0:1024].rearrange("p (b q) -> p b q", q=128),
                    AX.X, ALU.max, apply_absolute_value=True),
                    waits=w_r, inc="vp")
    VMF = P.add("vector", lambda e: e.tensor_reduce(
        mqMX[0:34, 0:16].unsqueeze(2), mqs[0:34, 0:16, 0:16],
        AX.X, ALU.max, apply_absolute_value=True),
        waits=[("vp", P.now("vp"))], inc="vp")
    P.add("scalar", lambda e: e.activation(
        mqMX2[0:34, :], mqMX[0:34, :], AF.Copy, bias=1.0),
        waits=[("vp", VMF)])
    SMF = P.add("scalar", lambda e: e.activation(
        mqMXb[0:34, :], mqMX2[0:34, :], AF.Ln), inc="sp")
    P.add("tensor", lambda e: e.matmul(
        PS[3][0:1, 0:16], ones1[0:34, 0:1], mqMXb[0:34, 0:16],
        start=True, stop=True, skip_group_check=True),
        waits=[("sp", SMF)], inc="tp")
    TSM = P.now("tp")
    SCP = P.add("scalar", lambda e: e.copy(cpr[0:1, :], PS[3][0:1, 0:16]),
                waits=[("tp", TSM)], inc="sp")
    SCS = P.add("scalar", lambda e: e.activation(
        cpos[0:1, :], cpr[0:1, :], AF.Exp, scale=0.25), inc="sp")
    P.add("tensor", lambda e: e.matmul(
        PS[3][:, 0:16], onesneg[0:1, :], cpos[0:1, 0:16],
        start=True, stop=True, skip_group_check=True),
        waits=[("sp", SCS)], inc="tp")
    TBR = P.now("tp")
    CB_S = P.add("scalar", lambda e: e.copy(biasrow[:, :], PS[3][:, 0:16]),
                 waits=[("tp", TBR)], inc="sp")
    if debug:
        tapnow("mqs", mqs[0:34, :, :], [("vp", VMF)])
        tapnow("mqMX", mqMX[0:34, :], [("sp", SMF)])
        tapnow("biasrow", biasrow[:, :], [("sp", CB_S)])
    if upto == "cbound":
        return _early_out("cbound")

    mir_done = [0] * 16
    headS_tp = [0] * 16
    oproj_tp = [0] * 16
    op_evac = [0] * NTT
    op_dma = [0] * NTT
    prev_pv = [0] * 8
    prev_exp7 = 0
    prev_SLR = 0
    for h in range(nheads):
        hb = h % 2
        w_m = [("sp", QT_DONE)]
        if h >= 2:
            w_m.append(("tp", headS_tp[h - 2]))
        P.add("sync", lambda e, h=h, hb=hb: e.dma_start(
            mirQ[hb][0:64, 0:8, :],
            QT_sb[64:128, 0:8, h * 128:(h + 1) * 128]),
            waits=w_m, inc="q_m", inc_k=16)
        P.add("sync", lambda e, h=h, hb=hb: e.dma_start(
            mirQ[hb][64:128, 0:8, :],
            QT_sb[0:64, 0:8, h * 128:(h + 1) * 128]),
            inc="q_m", inc_k=16)
        mir_done[h] = P.now("q_m")

        for G in range(2):
            G4 = G * 4
            BK = slice(h * 128, (h + 1) * 128)
            exp_done = [0] * 8
            pv_done = [0] * 8

            def emit_S(kc):
                w_s = []
                if kc == 0:
                    w_s = [("sp", prev_exp7)]
                    if G == 0:
                        w_s.append(("q_m", mir_done[h]))
                        if h == 0:
                            w_s += [("sp", PROJ_S), ("sp", CB_S)]
                else:
                    w_s = [("sp", exp_done[kc - 1])]
                P.add("tensor", lambda e, kc=kc, BK=BK, G4=G4: e.matmul(
                    PS[0][:, 0:512], KT_sb[0:64, kc, BK],
                    QT_sb[0:64, G4:G4 + 4, BK],
                    start=True, stop=True, tile_position=(0, 0),
                    skip_group_check=True), waits=w_s)
                P.add("tensor", lambda e, kc=kc, BK=BK, G4=G4, hb=hb: e.matmul(
                    PS[3][:, 0:512], KT_sb[64:128, kc, BK],
                    mirQ[hb][64:128, G4:G4 + 4, :],
                    start=True, stop=True, tile_position=(64, 0),
                    skip_group_check=True))
                P.add("tensor", lambda e, kc=kc, BK=BK, G4=G4, hb=hb: e.matmul(
                    PS[0][:, 512:1024], KT_sb[0:64, kc, BK],
                    mirQ[hb][0:64, G4:G4 + 4, :],
                    start=True, stop=True, tile_position=(0, 0),
                    skip_group_check=True))
                P.add("tensor", lambda e, kc=kc, BK=BK, G4=G4: e.matmul(
                    PS[3][:, 512:1024], KT_sb[64:128, kc, BK],
                    QT_sb[64:128, G4:G4 + 4, BK],
                    start=True, stop=True, tile_position=(64, 0),
                    skip_group_check=True), inc="tp")
                TS = P.now("tp")
                w_e = [("tp", TS)]
                if kc >= 2:
                    w_e.append(("tp", pv_done[kc - 2]))
                else:
                    w_e.append(("tp", prev_pv[kc + 6]))
                par = kc % 2
                P.add("scalar", lambda e, par=par, h=h: e.activation(
                    P_A[par][:, :], PS[0][:, 0:1024], AF.Exp,
                    bias=biasrow[:, h:h + 1]), waits=w_e)
                exp_done[kc] = P.add("scalar", lambda e, par=par, h=h: e.activation(
                    P_B[par][:, :], PS[3][:, 0:1024], AF.Exp,
                    bias=biasrow[:, h:h + 1]), inc="sp")

            def emit_PV(kc):
                par = kc % 2
                w_p = [("sp", exp_done[kc])]
                if kc == 0:
                    w_p.append(("sp", prev_SLR))
                    if h == 0 and G == 0:
                        w_p.append(("g_i", GI_V))
                for sl in range(2):
                    P.add("tensor", lambda e, kc=kc, sl=sl, par=par, h=h: e.matmul(
                        PS[2][0:65, sl * 512:(sl + 1) * 512],
                        Vtm[:, h, (2 * kc) * 65:(2 * kc) * 65 + 65],
                        P_A[par][:, sl * 512:(sl + 1) * 512],
                        start=(kc == 0), stop=False,
                        tile_position=(0, 0), skip_group_check=True),
                        waits=w_p if sl == 0 else [])
                    P.add("tensor", lambda e, kc=kc, sl=sl, par=par, h=h: e.matmul(
                        PS[2][0:65, sl * 512:(sl + 1) * 512],
                        Vtm[:, h, (2 * kc + 1) * 65:(2 * kc + 1) * 65 + 65],
                        P_B[par][:, sl * 512:(sl + 1) * 512],
                        start=False, stop=(kc == 7),
                        tile_position=(0, 0), skip_group_check=True),
                        inc="tp" if sl == 1 else None)
                pv_done[kc] = P.now("tp")

            for kc in range(8):
                emit_S(kc)
                if kc >= 1:
                    emit_PV(kc - 1)
            emit_PV(7)
            if debug and h == 0 and G == 1:
                tapnow("P_A0", P_A[1][:, :], [("sp", exp_done[1])])
                tapnow("P_B0", P_B[1][:, :], [("sp", exp_done[1])])

            SAR = P.add("scalar", lambda e: e.copy(
                araw[0:64, :], PS[2][0:64, 0:1024]),
                waits=[("tp", pv_done[7])], inc="sp")
            VREC = P.add("vector", lambda e: e.reciprocal(
                lrecf[64:65, :], PS[2][64:65, 0:1024]),
                waits=[("tp", pv_done[7]), ("g_i", GI_L)], inc="vp")
            SLB = P.add("scalar", lambda e: e.copy(
                lABb[0:1, :], lrecf[64:65, :]), waits=[("vp", VREC)], inc="sp")
            prev_SLR = SLB
            w_lt = [("sp", SLB)]
            if G == 0 and h > 0:
                w_lt.append(("sp", op_evac[h - 1]))
            for qt in range(8):
                w_t = w_lt if qt == 0 else []
                P.add("tensor", lambda e, qt=qt: e.transpose(
                    PSb[1][:, qt * 64:qt * 64 + 33],
                    lABb[0:33, qt * 128:(qt + 1) * 128], ident_s[0:33, 0:33]),
                    waits=w_t, inc="tp" if qt == 7 else None)
            TLT = P.now("tp")
            VRT = P.add("vector", lambda e: e.tensor_copy(
                rT_f[:, 0:8],
                PSb[1][:, 0:512].rearrange("p (q e) -> p q e", e=64)[:, :, 0:1].squeeze(2)),
                waits=[("tp", TLT)], inc="vp")
            for j in range(8):
                w_t = [("sp", SAR)] if j == 0 else []
                P.add("tensor", lambda e, j=j: e.transpose(
                    PSb[1][:, 1024 + j * 64:1024 + (j + 1) * 64],
                    araw[0:64, j * 128:(j + 1) * 128], ident_s[0:64, 0:64]),
                    waits=w_t, inc="tp" if j == 7 else None)
            TAT = P.now("tp")
            for j in range(8):
                w_v = [("tp", TAT), ("vp", VRT)] if j == 0 else []
                P.add("vector", lambda e, j=j: e.tensor_scalar(
                    tnorm[:, j * 64:(j + 1) * 64],
                    PSb[1][:, 1024 + j * 64:1024 + (j + 1) * 64],
                    rT_f[:, j:j + 1], None, ALU.mult),
                    waits=w_v, inc="vp" if j == 7 else None)
            VTN = P.now("vp")
            for ocj in range(4):
                c0 = 256 + ocj * 128 if ocj < 2 else 512 + ocj * 128
                w_t = [("vp", VTN)] if ocj == 0 else []
                P.add("tensor", lambda e, ocj=ocj, c0=c0: e.matmul(
                    PS[1][0:64, c0:c0 + 128],
                    tnorm[:, ocj * 64:(ocj + 1) * 64], ident_s[:, :],
                    start=True, stop=True, tile_position=(0, 0),
                    skip_group_check=True), waits=w_t)
                P.add("tensor", lambda e, ocj=ocj, c0=c0: e.matmul(
                    PS[1][64:128, c0:c0 + 128],
                    tnorm[:, 256 + ocj * 64:256 + (ocj + 1) * 64], ident_s[:, :],
                    start=True, stop=True, tile_position=(0, 64),
                    skip_group_check=True),
                    inc="tp" if ocj == 3 else None)
            TNB = P.now("tp")
            w_a = [("tp", TNB)]
            if G == 0 and h >= 1:
                w_a.append(("g_o", op_dma[h - 1]))
                if h >= 2:
                    w_a.append(("tp", oproj_tp[h - 2]))
            P.add("scalar", lambda e, hb=hb, G4=G4: e.copy(
                aT_h[hb][:, G4:G4 + 2, :],
                PS[1][:, 256:512].rearrange("p (j q) -> p j q", q=128)),
                waits=w_a)
            tail_sp = P.add("scalar", lambda e, hb=hb, G4=G4: e.copy(
                aT_h[hb][:, G4 + 2:G4 + 4, :],
                PS[1][:, 768:1024].rearrange("p (j q) -> p j q", q=128)),
                inc="sp")
            prev_pv = pv_done
            prev_exp7 = exp_done[7]
            if debug and h == 0 and G == 1:
                tapnow("araw0", araw[0:64, :], [("sp", tail_sp)])
                tapnow("rT0", rT_f[:, 0:8], [("vp", VRT)])
                tapnow("tnorm0", tnorm[:, :], [("sp", tail_sp)])
        headS_tp[h] = pv_done[7]
        for hh in range(2):
            for oc in range(8):
                w_m2 = []
                if hh == 0 and oc == 0:
                    w_m2 = [("sp", tail_sp), ("gw", GW_WO)]
                P.add("tensor", lambda e, hb=hb, hh=hh, oc=oc: e.matmul(
                    PS[1][:, hh * 512:(hh + 1) * 512],
                    aT_h[hb][:, oc, :],
                    woT_sb[:, oc, hh * 512:(hh + 1) * 512],
                    start=(oc == 0), stop=(oc == 7)),
                    waits=w_m2, inc="tp" if (hh == 1 and oc == 7) else None)
        oproj_tp[h] = P.now("tp")
        op_evac[h] = P.add("scalar", lambda e, hb=hb: e.copy(
            aT_h[1 - hb][:, 0:8, :],
            PS[1][:, 0:1024].rearrange("p (j q) -> p j q", q=128)),
            waits=[("tp", oproj_tp[h])], inc="sp")
        op_dma[h] = P.add("gpsimd", lambda e, h=h, hb=hb: e.dma_start(
            ao_hbm[h * 128:(h + 1) * 128, :].rearrange("p (j q) -> p j q", q=128),
            aT_h[1 - hb][:, 0:8, :]),
            waits=[("sp", op_evac[h])], inc="g_o", inc_k=16)
        if debug and h == 0:
            tapnow("aTh0", aT_h[0][:, 0:8, :], [("sp", op_evac[0])])
    B2_S = op_evac[nheads - 1]
    B2_T = P.now("tp")
    if upto == "attn":
        return _early_out("attn")

    P.add("sync", lambda e: e.dma_start(xqT[:, 0:8, :], xqT_hbm[:, 0:8, :]),
          waits=[("sp", B2_S)], inc="q_f", inc_k=16)
    P.add("sync", lambda e: e.dma_start(xqlT[:, 0:8, :], xql_hbm[:, 0:8, :]),
          inc="q_f", inc_k=16)

    chunk_ld = [0] * len(CHUNKS)
    chunk_td = [0] * len(CHUNKS)
    acc_vp = [[0] * NTT for _ in range(len(CHUNKS))]
    _chunks = CHUNKS if nchunks is None else CHUNKS[:nchunks]
    chunk_lg = [0] * len(CHUNKS)
    vhH, vhqH, sheH, thtH, accH = [], [], [], [], []
    for c, csz in enumerate(_chunks):
        b = c % 2
        off = CHUNK_OFF[c]
        nj = csz // 128
        w_l = [("gw", B1_G)]
        if c <= 1:
            w_l.append(("sp", B2_S))
            w_l.append(("tp", B2_T))
        if c >= 1:
            w_l.append(("tp", chunk_td[c - 1]))
        P.add("sync", lambda e, off=off, csz=csz: e.dma_start(
            wgq_sb[:, 0:8, 0:csz], wgq_hbm[:, 0:8, off:off + csz]),
            waits=w_l, inc="q_f", inc_k=16)
        P.add("sync", lambda e, off=off, csz=csz: e.dma_start(
            wgql_sb[:, 0:8, 0:csz], wgql_hbm[:, 0:8, off:off + csz]),
            inc="q_f", inc_k=16)
        w_d = []
        if c >= 2:
            w_d.append(("tp", chunk_td[c - 2]))
        P.add("sync", lambda e, b=b, off=off, nj=nj: e.dma_start(
            wdq_sb[b][:, 0:nj, :],
            wdq_hbm[off // 128:off // 128 + nj, :, :].transpose([1, 0, 2])),
            waits=w_d, inc="q_f", inc_k=16)
        chunk_ld[c] = P.now("q_f")
        P.add("gpsimd", lambda e, off=off, csz=csz: e.dma_start(
            wuq_sb[:, 0:8, 0:csz], wuq_hbm[:, 0:8, off:off + csz]),
            waits=w_l, inc="gw", inc_k=16)
        P.add("gpsimd", lambda e, off=off, csz=csz: e.dma_start(
            wuql_sb[:, 0:8, 0:csz], wuql_hbm[:, 0:8, off:off + csz]),
            inc="gw", inc_k=16)
        chunk_lg[c] = P.now("gw")

        def stage1(tt, c=c, b=b, csz=csz):
            gidx = c * NTT + tt
            psg = PS[gidx % 2 * 2]
            for hc in range(8):
                for ps in range(3):
                    w_m = []
                    if hc == 0 and ps == 0:
                        w_m = [("q_f", chunk_ld[c]), ("gw", chunk_lg[c])]
                        if gidx >= 2:
                            w_m.append(("vp", vhH[gidx - 2]))
                    xop = xqlT if ps == 2 else xqT
                    wop = wgql_sb if ps == 1 else wgq_sb
                    P.add("tensor", lambda e, hc=hc, tt=tt, csz=csz, xop=xop, wop=wop, ps=ps, psg=psg: e.matmul(
                        psg[:, 0:csz], xop[:, hc, tt * 128:(tt + 1) * 128],
                        wop[:, hc, 0:csz],
                        start=(hc == 0 and ps == 0), stop=(hc == 7 and ps == 2)),
                        waits=w_m, inc="tp" if (hc == 7 and ps == 2) else None)
            TG = P.now("tp")
            w_si = [("tp", TG)]
            if gidx >= 1:
                w_si.append(("vp", vhH[gidx - 1]))
            SSI = P.add("scalar", lambda e, csz=csz, psg=psg: e.activation(
                sg_sb[:, 0:csz], psg[:, 0:csz], AF.Silu),
                waits=w_si, inc="sp")
            for hc in range(8):
                for ps in range(3):
                    xop = xqlT if ps == 2 else xqT
                    wop = wuql_sb if ps == 1 else wuq_sb
                    P.add("tensor", lambda e, hc=hc, tt=tt, csz=csz, xop=xop, wop=wop, ps=ps, psg=psg: e.matmul(
                        psg[:, 512:512 + csz], xop[:, hc, tt * 128:(tt + 1) * 128],
                        wop[:, hc, 0:csz],
                        start=(hc == 0 and ps == 0), stop=(hc == 7 and ps == 2)),
                        inc="tp" if (hc == 7 and ps == 2) else None)
            TU = P.now("tp")
            VH = P.add("vector", lambda e, csz=csz, psg=psg: e.tensor_tensor(
                h_sb[:, 0:csz], sg_sb[:, 0:csz], psg[:, 512:512 + csz],
                ALU.mult), waits=[("sp", SSI), ("tp", TU)], inc="vp")
            vhH.append(VH)

        def stage1b(tt, c=c):
            wl = [("tp", thtH[-1])] if thtH else []
            VHQ = qd(h_sb, hq_sb[:, 0:512], hM, hS, hR, hT_sb, hA_sb, hQ_sb,
                     512, waits=[], waits_last=wl)
            vhqH.append(VHQ)

        def stage2(tt, c=c, b=b, csz=csz, nj=nj):
            gidx = c * NTT + tt
            for j in range(nj):
                w_t = []
                if j == 0:
                    w_t = [("vp", vhqH[gidx])]
                    if gidx >= 1:
                        w_t.append(("sp", sheH[gidx - 1]))
                P.add("tensor", lambda e, j=j: e.transpose(
                    PSb[3][:, j * 128:(j + 1) * 128],
                    hq_sb[:, j * 128:(j + 1) * 128], ident_s[:, :]),
                    waits=w_t, inc="tp" if j == nj - 1 else None)
            THT = P.now("tp")
            thtH.append(THT)
            SHE = P.add("scalar", lambda e, csz=csz: e.copy(
                hqT_sb[:, 0:csz], PSb[3][:, 0:csz]),
                waits=[("tp", THT)], inc="sp")
            sheH.append(SHE)
            for hh in range(2):
                for j in range(nj):
                    w_m = []
                    if hh == 0 and j == 0:
                        w_m = [("sp", SHE)]
                        if gidx >= 1:
                            w_m.append(("vp", accH[gidx - 1]))
                    P.add("tensor", lambda e, hh=hh, j=j, b=b, nj=nj: e.matmul(
                        PS[1][:, hh * 512:(hh + 1) * 512],
                        hqT_sb[:, j * 128:(j + 1) * 128],
                        wdq_sb[b][:, j, hh * 512:(hh + 1) * 512],
                        start=(j == 0), stop=(j == nj - 1)),
                        waits=w_m,
                        inc="tp" if (hh == 1 and j == nj - 1) else None)
            TD = P.now("tp")
            if c == 0:
                acc_vp[c][tt] = P.add("vector", lambda e, tt=tt: e.tensor_copy(
                    ffn_acc[:, tt * H:(tt + 1) * H], PS[1][:, 0:1024]),
                    waits=[("tp", TD)], inc="vp")
            else:
                acc_vp[c][tt] = P.add("vector", lambda e, tt=tt: e.tensor_tensor(
                    ffn_acc[:, tt * H:(tt + 1) * H],
                    ffn_acc[:, tt * H:(tt + 1) * H], PS[1][:, 0:1024],
                    ALU.add), waits=[("tp", TD)], inc="vp")
            accH.append(acc_vp[c][tt])
            chunk_td[c] = P.now("tp")

        for tt in range(NTT):
            stage1(tt)
            if tt >= 1:
                stage2(tt - 1)
            stage1b(tt)
        stage2(NTT - 1)
    NCH = len(_chunks)
    tapnow("ffn0", ffn_acc[:, 0:1024], [("vp", acc_vp[NCH - 1][0])])
    if upto == "b3":
        return _early_out("b3")

    B3_V = P.now("vp")
    vo_hist = [0] * NTT
    st_hist = [0] * NTT
    for tt in range(NTT):
        b = tt % 2
        w = [("vp", B3_V)] if tt < 2 else [("vp", vo_hist[tt - 2]),
                                           ("g_st", st_hist[tt - 2])]
        P.add("sync", lambda e, tt=tt, b=b: e.dma_start(
            xr_t[b][:, :], x_in[tt * 128:(tt + 1) * 128, :]),
            waits=w, inc="q_r", inc_k=16)
        P.add("sync", lambda e, tt=tt, b=b: e.dma_start(
            ao_t[b][:, :], ao_hbm[tt * 128:(tt + 1) * 128, :]),
            waits=[("g_o", op_dma[tt])], inc="q_r", inc_k=16)
        QR = P.now("q_r")
        wv = [("q_r", QR), ("vp", acc_vp[NCH - 1][tt]), ("sp", op_evac[tt])]
        if tt >= 2:
            wv.append(("g_st", st_hist[tt - 2]))
        P.add("vector", lambda e, tt=tt, b=b: e.tensor_tensor(
            ob_t[b][:, :], ffn_acc[:, tt * H:(tt + 1) * H], xr_t[b][:, :],
            ALU.add), waits=wv)
        VO = P.add("vector", lambda e, tt=tt, b=b: e.tensor_tensor(
            ob_t[b][:, :], ob_t[b][:, :], ao_t[b][:, :],
            ALU.add), inc="vp")
        vo_hist[tt] = VO
        st_hist[tt] = P.add("gpsimd", lambda e, tt=tt, b=b: e.dma_start(
            y_out[tt * 128:(tt + 1) * 128, :], ob_t[b][:, :]),
            waits=[("vp", VO)], inc="g_st", inc_k=16)
    P.add("gpsimd", lambda e: e.wait_ge(P.sems["g_st"], NTT * 16))

    P.emit()
    nc.compile()
    return nc


_NC_CACHE = {}


def _get_nc():
    if "nc" not in _NC_CACHE:
        _NC_CACHE["nc"] = build()
    return _NC_CACHE["nc"]


def kernel(L0_attn_x, L0_attn_wq, L0_attn_wk, L0_attn_wv, L0_attn_wo,
           L0_ffn_w_gate, L0_ffn_w_up, L0_ffn_w_down):
    nc = _get_nc()
    x = np.ascontiguousarray(np.asarray(L0_attn_x, dtype=np.float32))
    wqT = np.ascontiguousarray(np.asarray(L0_attn_wq, dtype=np.float32).T)
    wkT = np.ascontiguousarray(np.asarray(L0_attn_wk, dtype=np.float32).T)
    wvT = np.ascontiguousarray(np.asarray(L0_attn_wv, dtype=np.float32).T)
    woT = np.ascontiguousarray(np.asarray(L0_attn_wo, dtype=np.float32).T)
    wg = np.ascontiguousarray(np.asarray(L0_ffn_w_gate, dtype=np.float32))
    wu = np.ascontiguousarray(np.asarray(L0_ffn_w_up, dtype=np.float32))
    wd = np.ascontiguousarray(np.asarray(L0_ffn_w_down, dtype=np.float32))
    ident = np.eye(128, dtype=ml_dtypes.bfloat16)
    in_maps = []
    for c in range(8):
        in_maps.append(dict(
            x_in=np.ascontiguousarray(x[c]),
            wqT=wqT, wkT=wkT, wvT=wvT, woT=woT,
            wg=wg, wu=wu, wd=wd, ident=ident,
        ))
    res = run_bass_kernel_spmd(nc, in_maps, core_ids=list(range(8)))
    out = np.stack([np.asarray(res.results[c]["y"], dtype=np.float32)
                    for c in range(8)], axis=0)
    return out
